# version 08
# baseline (speedup 1.0000x reference)
import numpy as np
import ml_dtypes
import concourse.bass as bass
import concourse.mybir as mybir
from concourse.bass_utils import run_bass_kernel_spmd

F32 = mybir.dt.float32
BF16 = mybir.dt.bfloat16
AF = mybir.ActivationFunctionType
ALU = mybir.AluOpType
AX = mybir.AxisListType
BF16NP = ml_dtypes.bfloat16

B, S, D = 16, 2048, 1024
NCORES = 8
SEQ_PER_CORE = B // NCORES
TOK = SEQ_PER_CORE * S
NH, DK = 16, 64
DH = 4096
NCH = D // 128
QR = 512
NQR = S // QR
LN_EPS = 1e-5
ATT_EPS = 1e-8
RD_SCALE = 1.0 / float(np.sqrt(np.float32(48.0)))
T_FFN = QR
PRECAST = True
LVDBG = False

_ENGINES = ("tensor", "vector", "scalar", "gpsimd", "sync")


class Prog:
    def __init__(self, nc):
        self.nc = nc
        self.queue = {e: [] for e in _ENGINES}
        self.cnt = {}
        self.sem_names = []
        self.writer = {}
        self.readers = {}
        self.satisfied = {}
        self.final_tokens = []
        self.exclusive = set()

    def _sem(self, name):
        if name not in self.cnt:
            self.cnt[name] = 0
            self.sem_names.append(name)
        return name

    @staticmethod
    def _key(r):
        if isinstance(r, str):
            return r
        n = getattr(r, "name", None)
        if n is None:
            raise TypeError(f"resource {r!r} has no name")
        return n

    def _waits_for(self, eng, ins, outs):
        need = {}

        def add(tok):
            if tok is None:
                return
            sem, val, is_dma, peng = tok
            if not is_dma and peng == eng and eng == "tensor":
                return
            if self.satisfied.get((eng, sem), 0) >= val:
                return
            if need.get(sem, 0) < val:
                need[sem] = val

        for r in ins:
            k = self._key(r)
            add(self.writer.get(k))
            if k in self.exclusive:
                for tok in self.readers.get(k, ()):
                    add(tok)
        for r in outs:
            k = self._key(r)
            add(self.writer.get(k))
            for tok in self.readers.get(k, ()):
                add(tok)
        for sem, val in need.items():
            self.satisfied[(eng, sem)] = val
        return list(need.items())

    def _record(self, tok, ins, outs):
        for r in ins:
            self.readers.setdefault(self._key(r), []).append(tok)
        for r in outs:
            k = self._key(r)
            self.writer[k] = tok
            self.readers[k] = []

    def op(self, eng, fn, ins=(), outs=(), sem=None):
        sem = self._sem(sem or ("s_" + eng))
        waits = self._waits_for(eng, ins, outs)
        self.cnt[sem] += 1
        tok = (sem, self.cnt[sem], False, eng)
        self._record(tok, ins, outs)
        self.queue[eng].append((waits, fn, sem, 1))
        return tok

    def dma(self, eng, out_ap, in_ap, ins=(), outs=(), sem=None, **kw):
        if sem is None:
            sem = "gd_" + self._key(outs[0] if outs else "misc")
        sem = self._sem(sem)
        waits = self._waits_for(eng, ins, outs)
        self.cnt[sem] += 16
        tok = (sem, self.cnt[sem], True, eng)
        self._record(tok, ins, outs)
        fn = (lambda E, o=out_ap, i=in_ap, k=dict(kw):
              E.dma_start(out=o, in_=i, **k))
        self.queue[eng].append((waits, fn, sem, 16))
        return tok

    def keep(self, tok):
        self.final_tokens.append(tok)

    def barrier(self, res_key, sem_name, eng="sync"):
        self.writer[res_key] = (self._sem(sem_name), self.cnt.get(sem_name, 0),
                                True, eng)

    def materialize(self, stack):
        nc = self.nc
        sems = {n: stack.enter_context(nc.semaphore(n)) for n in self.sem_names}
        block = stack.enter_context(nc.Block())
        fin = {}
        for sem, val, _d, _e in self.final_tokens:
            fin[sem] = max(fin.get(sem, 0), val)
        for sem, val in fin.items():
            self.queue["sync"].append(([(sem, val)], None, None, 0))

        def run(E, items):
            for waits, fn, sem, inc in items:
                for s, v in waits:
                    E.wait_ge(sems[s], v)
                if fn is not None:
                    inst = fn(E)
                    if inc:
                        inst.then_inc(sems[sem], inc)

        for ename in _ENGINES:
            items = self.queue[ename]
            if not items:
                continue
            getattr(block, ename)(lambda E, it=items: run(E, it))


def _strips(w):
    O, I = w.shape
    t = np.asarray(w).T.reshape(I // 128, 128, O // 128, 128)
    return np.ascontiguousarray(t.transpose(2, 1, 0, 3).reshape(O // 128, 128, I))


def _pmaj(v):
    return np.ascontiguousarray(np.asarray(v).reshape(-1, 128).T)


def _rope_perm():
    p = np.arange(D).reshape(NH, DK)
    p = np.concatenate([p[:, 0::2], p[:, 1::2]], axis=1)
    return p.reshape(-1)


def _rope_tables():
    pos = np.arange(S, dtype=np.float64)
    k = np.arange(DK // 2, dtype=np.float64)
    ang = pos[None, :] / (10000.0 ** (2.0 * k[:, None] / DK))
    c32, s32 = np.cos(ang), np.sin(ang)
    C = np.tile(c32, (4, 1)).astype(np.float32).astype(BF16NP)
    Sg = np.concatenate([-s32, s32, -s32, s32], 0).astype(np.float32).astype(BF16NP)
    return C, Sg


def _pswap():
    P = np.zeros((128, 128), np.float32)
    for i in range(128):
        P[i, (i // 64) * 64 + (i % 64 + 32) % 64] = 1.0
    return P.astype(BF16NP)


def _l0_masks():
    p = np.arange(128)[:, None]
    j = np.arange(QR)[None, :]
    return [np.where(j < 128 * t + p, np.float32(-1e30), np.float32(0.0)).astype(np.float32)
            for t in range(4)]


def _l1_masks():
    sc = np.float32(1.0) / np.sqrt(np.float32(D * S))
    p = np.arange(128)[:, None]
    j = np.arange(QR)[None, :]
    return [np.where(j >= 128 * t + p, sc, np.float32(0.0)).astype(np.float32)
            for t in range(4)]


def _sel64():
    s = np.zeros((64, 128), np.float32)
    s[0, 0:64] = 1.0
    s[32, 64:128] = 1.0
    return s


def host_prepare(inp):
    d = {}
    perm = _rope_perm()
    d["wqT"] = _strips(np.asarray(inp["L0_attn_Wq"])[perm])
    d["wkT"] = _strips(np.asarray(inp["L0_attn_Wk"])[perm])
    d["wvT"] = np.ascontiguousarray(np.asarray(inp["L0_attn_Wv"]).T)
    d["woT"] = _strips(inp["L0_attn_Wo"])
    d["bq"] = np.asarray(inp["L0_attn_bq"])[perm].reshape(1, D).copy()
    d["bk"] = np.asarray(inp["L0_attn_bk"])[perm].reshape(1, D).copy()
    d["bv"] = np.asarray(inp["L0_attn_bv"]).reshape(1, D).copy()
    d["bo"] = np.asarray(inp["L0_attn_bo"]).reshape(1, D).copy()
    d["ropeC"], d["ropeS"] = _rope_tables()
    d["pswap"] = _pswap()
    p_ = np.arange(128)[:, None]
    j_ = np.arange(128)[None, :]
    d["tril128"] = np.where(j_ < p_, np.float32(-1e30), np.float32(0.0)).astype(np.float32)
    d["bvrow0"] = np.asarray(inp["L0_attn_bv"]).reshape(1, D)[:, 0:QR].copy()
    d["bvrow1"] = np.asarray(inp["L0_attn_bv"]).reshape(1, D)[:, QR:D].copy()
    d["sel64"] = _sel64()
    d["w1T"] = _strips(inp["L0_ffn_w1_w"])
    d["w2T"] = _strips(inp["L0_ffn_w2_w"])
    d["h1T"] = _strips(inp["L0_ffn_h1_w"])
    d["h2T"] = _strips(inp["L0_ffn_h2_w"])
    d["w3T"] = _strips(inp["L0_ffn_w3_w"])
    d["fs1"] = _pmaj(inp["L0_ffn_w1_s"])
    d["fs2"] = _pmaj(inp["L0_ffn_w2_s"])
    d["fsh1"] = _pmaj(inp["L0_ffn_h1_s"])
    d["fsh2"] = _pmaj(inp["L0_ffn_h2_s"])
    d["fs3"] = _pmaj(inp["L0_ffn_w3_s"])
    for nm in ("q", "k", "v"):
        d[f"w1{nm}T"] = _strips(np.asarray(inp[f"L1_attn_W1{nm}"]).T)
        d[f"w2{nm}T"] = _strips(np.asarray(inp[f"L1_attn_W2{nm}"]).T) if nm != "v" \
            else np.ascontiguousarray(np.asarray(inp["L1_attn_W2v"]))
        d[f"b1{nm}"] = np.asarray(inp[f"L1_attn_b1{nm}"]).reshape(1, D).copy()
        d[f"b2{nm}"] = np.asarray(inp[f"L1_attn_b2{nm}"]).reshape(1, D).copy()
    d["ln_w"] = _pmaj(inp["L1_attn_ln_w"])
    d["ln_b"] = _pmaj(inp["L1_attn_ln_b"])
    for t, m in enumerate(_l1_masks()):
        d[f"mask1_{t}"] = m
    d["qprojT"] = _strips(inp["L1_ffn_q_proj_w"])
    d["routerT"] = np.ascontiguousarray(np.asarray(inp["L1_ffn_router_w"]).T)
    d["akT"] = np.ascontiguousarray(np.asarray(inp["L1_ffn_aux_keys"]).T)
    d["auxv"] = np.ascontiguousarray(np.asarray(inp["L1_ffn_aux_values"]).reshape(32, 128, D))
    d["relm"] = np.asarray(inp["L1_ffn_reliability_mask"]).reshape(1, 4096).copy()
    d["uw1T"] = _strips(inp["L1_ffn_unc_w1"])
    d["uw2T"] = np.ascontiguousarray(np.asarray(inp["L1_ffn_unc_w2"]).reshape(256, 1))
    d["ub1"] = np.asarray(inp["L1_ffn_unc_b1"]).reshape(1, 256).copy()
    d["ub2"] = np.asarray(inp["L1_ffn_unc_b2"]).reshape(1, 1).copy()
    d["gw1"] = np.asarray(inp["L1_ffn_gate_w1"], np.float32).reshape(1, 1)
    d["gbias"] = np.asarray(inp["L1_ffn_gate_bias"], np.float32).reshape(1, 1)
    d["identf"] = np.eye(128, dtype=np.float32)
    return d


def _np_dt(a):
    return BF16 if a.dtype == BF16NP else F32


def declare_inputs(nc, host_arrays, names):
    aps = {}
    for n in names:
        a = host_arrays[n]
        aps[n] = nc.dram_tensor(n, list(a.shape), _np_dt(a), kind="ExternalInput").ap()
    return aps


class WS:
    pass


def make_consts(p, ctx):
    nc = p.nc
    c = WS()
    c.ones_row_bf = ctx.enter_context(nc.sbuf_tensor("c1r_bf", [1, QR], BF16))
    c.ones_col_bf = ctx.enter_context(nc.sbuf_tensor("c1c_bf", [128, 1], BF16))
    c.ones_row_f = ctx.enter_context(nc.sbuf_tensor("c1r_f", [1, 128], F32))
    for t in (c.ones_row_bf, c.ones_col_bf, c.ones_row_f):
        p.op("vector", lambda E, tt=t: E.memset(tt[:], 1.0), outs=[t])
    c._fcache = {}
    c._fctx = ctx
    c._prog = p

    def constf(val):
        val = float(val)
        if val not in c._fcache:
            t = c._fctx.enter_context(
                nc.sbuf_tensor(f"cf_{len(c._fcache)}", [128, 1], F32))
            c._prog.op("vector", lambda E, tt=t, v=val: E.memset(tt[:], v),
                       outs=[t])
            c._fcache[val] = t
        return c._fcache[val]

    c.constf = constf
    return c


def sbuf_const(p, ctx, name, dram_ap, dtype=None):
    shp = list(dram_ap.shape)
    assert len(shp) == 2
    dtype = dtype or dram_ap.dtype
    t = ctx.enter_context(p.nc.sbuf_tensor("k_" + name, shp, dtype))
    q = "gpsimd" if dtype != dram_ap.dtype else "scalar"
    p.dma(q, t[:], dram_ap, ins=[name + "@hbm"], outs=[t])
    return t


def emit_ln(p, ws, cst, x_dram, seq, xn_bf=None, xn32_dram=None, lv_acc=None,
            tag="ln", src_key="x@hbm", lv_slots=None, lv_dram=None,
            xnbf_dram=None):
    nc = p.nc
    rows = ws.rows
    ps_s, ps_q, ps_mu, ps_r = ws.ps[0], ws.ps[1], ws.ps[2], ws.ps[3]
    stats_only = xn_bf is None and xn32_dram is None and xnbf_dram is None
    for r in range(NQR):
        qs = seq * S + r * QR
        mu = rows[0:1, 0:QR]
        var = rows[0:1, QR:2 * QR]
        ex2 = var
        rstd = rows[0:1, 2 * QR:3 * QR]
        tmp = rstd
        onesf = cst.constf(1.0)
        for c in range(NCH):
            t32 = ws.x32[c % 2]
            sq = ws.sqf[c % 2]
            p.dma("sync", t32[:], x_dram[c * 128:(c + 1) * 128, qs:qs + QR],
                  ins=[src_key], outs=[t32])
            p.op("vector", lambda E, a=t32, o=sq: E.tensor_mul(o[:], a[:], a[:]),
                 ins=[t32], outs=[sq])
            p.op("tensor", lambda E, a=t32, st=(c == 0), sp=(c == NCH - 1):
                 E.matmul(ps_s[0:1, 0:QR], onesf[:], a[:], start=st, stop=sp),
                 ins=[t32, onesf], outs=[ps_s])
            p.op("tensor", lambda E, a=sq, st=(c == 0), sp=(c == NCH - 1):
                 E.matmul(ps_q[0:1, 0:QR], onesf[:], a[:], start=st, stop=sp),
                 ins=[sq, onesf], outs=[ps_q])
        RK = "rows"
        p.op("vector", lambda E: E.tensor_scalar_mul(mu, ps_s[0:1, 0:QR], 1.0 / D),
             ins=[ps_s], outs=[RK])
        p.op("vector", lambda E: E.tensor_scalar_mul(ex2, ps_q[0:1, 0:QR], 1.0 / D),
             ins=[ps_q], outs=[RK])
        p.op("vector", lambda E: E.tensor_mul(tmp, mu, mu), ins=[RK], outs=[RK])
        p.op("vector", lambda E: E.tensor_sub(var, ex2, tmp), ins=[RK], outs=[RK])
        eps_t = cst.constf(LN_EPS)
        mhalf = cst.constf(-0.5)
        p.op("scalar", lambda E: E.activation(tmp, var, AF.Ln, bias=eps_t[0:1, 0:1]),
             ins=[RK, eps_t], outs=[RK])
        p.op("scalar", lambda E: E.activation(rstd, tmp, AF.Exp, scale=mhalf[0:1, 0:1]),
             ins=[RK, mhalf], outs=[RK])
        if not stats_only:
            p.op("tensor", lambda E: E.matmul(ps_mu[:, 0:QR], cst.ones_row_f[0:1, :],
                                              mu, start=True, stop=True),
                 ins=[RK, cst.ones_row_f], outs=[ps_mu])
            p.op("tensor", lambda E: E.matmul(ps_r[:, 0:QR], cst.ones_row_f[0:1, :],
                                              rstd, start=True, stop=True),
                 ins=[RK, cst.ones_row_f], outs=[ps_r])
        if lv_acc is not None:
            p.op("vector", lambda E: E.tensor_scalar_add(tmp, var, LN_EPS),
                 ins=[RK, ps_r], outs=[RK])
            p.op("vector", lambda E: E.reciprocal(tmp, tmp), ins=[RK], outs=[RK])
            p.op("vector", lambda E: E.tensor_mul(tmp, var, tmp), ins=[RK], outs=[RK])
            if lv_slots is not None:
                onet = cst.constf(1.0)
                p.op("scalar", lambda E, b=onet: E.activation(tmp, tmp, AF.Ln, bias=b[0:1, 0:1]),
                     ins=[RK, onet], outs=[RK])
                p.op("vector", lambda E, sl=lv_slots[r]: E.reduce_sum(
                    sl, tmp, axis=AX.X), ins=[RK], outs=[lv_acc])
            else:
                onet = cst.constf(1.0)
                p.op("scalar", lambda E, b=onet:
                     E.activation(lv_acc[0:1, 0:QR], tmp, AF.Ln, bias=b[0:1, 0:1]),
                     ins=[RK, onet], outs=[lv_acc])
                tok = p.dma("sync", lv_dram[0:1, qs:qs + QR], lv_acc[0:1, 0:QR],
                            ins=[lv_acc], outs=[f"lv@hbm:{qs}"], sem="lvw")
                p.keep(tok)
        if stats_only:
            continue
        for c in range(NCH):
            t32 = ws.x32[2 + c % 2]
            p.dma("sync", t32[:], x_dram[c * 128:(c + 1) * 128, qs:qs + QR],
                  ins=[src_key], outs=[t32])
            p.op("vector", lambda E, a=t32: E.tensor_sub(a[:], a[:], ps_mu[:, 0:QR]),
                 ins=[t32, ps_mu], outs=[t32])
            if xn32_dram is not None:
                t2 = ws.xtmp
                p.op("vector", lambda E, a=t32, o=t2:
                     E.tensor_mul(o[:], a[:], ps_r[:, 0:QR]),
                     ins=[t32, ps_r], outs=[t2])
                if xn_bf is not None:
                    p.op("vector", lambda E, o=t2, dd=xn_bf[c], rr=r:
                         E.tensor_copy(dd[:, rr * QR:(rr + 1) * QR], o[:]),
                         ins=[t2], outs=[xn_bf[c]])
                tok = p.dma("sync", xn32_dram[c * 128:(c + 1) * 128, qs:qs + QR],
                            t2[:], ins=[t2], outs=[f"xn32@hbm:{qs}:{c}"], sem="xnw")
                p.keep(tok)
            elif xnbf_dram is not None:
                dd = ws.xnb[c]
                p.op("vector", lambda E, a=t32, o=dd:
                     E.tensor_mul(o, a[:], ps_r[:, 0:QR]),
                     ins=[t32, ps_r], outs=[ws.xnbt[c]])
                tok = p.dma("sync", xnbf_dram[c * 128:(c + 1) * 128, qs:qs + QR],
                            dd, ins=[ws.xnbt[c]],
                            outs=[f"xn0:{seq}:{r * QR}:{c}"], sem="xn0w")
                p.keep(tok)
            else:
                p.op("vector", lambda E, a=t32, dd=xn_bf[c], rr=r:
                     E.tensor_mul(dd[:, rr * QR:(rr + 1) * QR], a[:], ps_r[:, 0:QR]),
                     ins=[t32, ps_r], outs=[xn_bf[c]])


def emit_l0_qkv(p, ws, cst, seq, v_dram, q_dram):
    nc = p.nc
    p.dma("scalar", ws.ropeC, ws.d_ropeC, ins=["ropeC@hbm"], outs=[ws.habt[14]])
    p.dma("scalar", ws.ropeS, ws.d_ropeS, ins=["ropeS@hbm"], outs=[ws.habt[15]])
    for pi, (proj, wT_dram, b_dram, dst) in enumerate((
            ("q", ws.d_wqT, ws.d_bq, None), ("k", ws.d_wkT, ws.d_bk, ws.Kfm))):
        for m in range(NCH):
            wb = ws.wstrip[m % 2]
            p.dma("gpsimd", wb[:, 0:D], wT_dram[m], ins=[f"wT{proj}@hbm"],
                  outs=[wb])
            bs = ws.bcur[m % 2]
            p.dma("gpsimd", bs[:], b_dram[0:1, m * 128:(m + 1) * 128],
                  ins=[f"b{proj}@hbm"], outs=[bs])
            for r in range(NQR):
                qs = r * QR
                ps = ws.ps[r % 2]
                for kc in range(NCH):
                    p.op("tensor", lambda E, w=wb, k2=kc, pp=ps, q=qs, st=(kc == 0):
                         E.matmul(pp[:, 0:QR], w[:, k2 * 128:(k2 + 1) * 128],
                                  ws.xn0[k2][:, q:q + QR], start=st, stop=False),
                         ins=[wb, ws.xn0[kc]], outs=[ps])
                p.op("tensor", lambda E, bb=bs, pp=ps:
                     E.matmul(pp[:, 0:QR], bb[:], cst.ones_row_bf[:],
                              start=False, stop=True),
                     ins=[bs, cst.ones_row_bf], outs=[ps])
                raw = ws.rraw[r % 2]
                p.op("scalar", lambda E, pp=ps, o=raw: E.copy(out=o, in_=pp[:, 0:QR]),
                     ins=[ps], outs=[ws.ut])
                psr = ws.ps[2 + r % 2]
                p.op("tensor", lambda E, o=psr, a=raw:
                     E.matmul(o[:, 0:QR], ws.pswap[:], a, start=True, stop=True),
                     ins=[ws.ut, ws.pswap], outs=[psr])
                t1 = ws.rt1
                p.op("vector", lambda E, a=raw, t=qs, o=t1:
                     E.tensor_mul(o[:], ws.ropeC[:, t:t + QR], a),
                     ins=[ws.ut, ws.habt[14]], outs=[t1])
                t2 = ws.rt2
                p.op("vector", lambda E, o=t2, pp=psr, t=qs:
                     E.tensor_mul(o[:], ws.ropeS[:, t:t + QR], pp[:, 0:QR]),
                     ins=[psr, ws.habt[15]], outs=[t2])
                if dst is None:
                    qstg = ws.u[2 + r % 2]
                    p.op("vector", lambda E, o=qstg, a=t1, bb2=t2:
                         E.tensor_add(o, a[:], bb2[:]),
                         ins=[t1, t2], outs=[qstg])
                    tok = p.dma("sync",
                                q_dram[m * 128:(m + 1) * 128, r * QR:(r + 1) * QR],
                                qstg, ins=[qstg], outs=[f"q0:{m}:{r}"],
                                sem=f"qw{r % 2}")
                    p.keep(tok)
                else:
                    p.op("vector", lambda E, dd=dst[m], rr2=r, a=t1, bb2=t2:
                         E.tensor_add(dd[:, rr2 * QR:(rr2 + 1) * QR], a[:], bb2[:]),
                         ins=[t1, t2], outs=[dst[m]])
    for f in range(2):
        p.dma("gpsimd", ws.d_bvrow[f], ws.d_bvrow_dram[f],
              ins=[f"bvrow{f}@hbm"], outs=["rows"])
    for kc in range(NCH):
        p.dma("gpsimd", ws.wvchunk[kc], ws.d_wvT[kc * 128:(kc + 1) * 128, :],
              ins=["wvT@hbm"], outs=[ws.wvk[kc]])
    for tt in range(S // 128):
        ts = tt * 128
        for fh in range(2):
            ps = ws.ps[4 + fh]
            vt = ws.ctx[2 * (tt % 2) + fh]
            for kc in range(NCH):
                p.op("tensor", lambda E, k2=kc, t=ts, f=fh, pp=ps, st=(kc == 0):
                     E.matmul(pp[:, 0:QR], ws.xn0[k2][:, t:t + 128],
                              ws.wvchunk[k2][:, f * QR:(f + 1) * QR],
                              start=st, stop=False),
                     ins=[ws.xn0[kc], ws.wvk[kc]], outs=[ps])
            p.op("tensor", lambda E, f=fh, pp=ps, bb=ws.d_bvrow:
                 E.matmul(pp[:, 0:QR], cst.ones_row_bf[0:1, 0:128],
                          bb[f], start=False, stop=True),
                 ins=["rows", cst.ones_row_bf], outs=[ps])
            p.op("scalar", lambda E, pp=ps, dd=vt:
                 E.copy(out=dd[:], in_=pp[:, 0:QR]),
                 ins=[ps], outs=[vt])
            tok = p.dma("sync", v_dram[ts:ts + 128, fh * QR:(fh + 1) * QR], vt[:],
                        ins=[vt], outs=[f"v0:{tt}"], sem=f"vstg{2 * (tt % 2) + fh}")
            p.keep(tok)
    for i in range(4):
        p.barrier(f"v0all{i}", f"vstg{i}")


def gen_l0_attn(p, ws, cst, seq, s0_dram, v_dram, q_dram, dbg=None):
    nc = p.nc
    psC, psL = ws.ps[4], ws.ps[5]
    psF = psO = psL
    for r in range(NQR):
        qs = r * QR
        nkt = (r + 1) * 4
        for pr in range(NH // 2):
            hA, hB = 2 * pr, 2 * pr + 1
            qt = ws.qsb[pr]
            p.dma("scalar", qt[:], q_dram[pr * 128:(pr + 1) * 128, qs:qs + QR],
                  ins=[f"q0:{pr}:{r}"], outs=[qt])
            for kt in range(nkt):
                psA = ws.ps[kt % 2]
                psB = ws.ps[2 + kt % 2]
                uA = ws.u[kt % 2]
                uB = ws.u[2 + kt % 2]
                ks = kt * 128
                if pr % 2 == 0:
                    vbw = ws.vbuf[kt]
                    p.dma("scalar", vbw[:],
                          v_dram[ks:ks + 128, pr * 128:(pr + 2) * 128],
                          ins=["v0all0", "v0all1", "v0all2", "v0all3",
                               f"v0:{kt}"],
                          outs=[f"vb{kt}"])
                vco = (pr % 2) * 128
                vbA = ws.vbuf[kt][:, vco:vco + 64]
                vbB = ws.vbuf[kt][:, vco + 64:vco + 128]
                p.op("tensor", lambda E, pp=psA, c=pr, k=ks, q2=qt:
                     E.matmul(pp[:, 0:QR], ws.Kfm[c][0:64, k:k + 128],
                              q2[0:64, 0:QR], start=True, stop=True,
                              tile_position=(0, 0)),
                     ins=[ws.Kfm[pr], qt], outs=[psA])
                p.op("tensor", lambda E, pp=psB, c=pr, k=ks, q2=qt:
                     E.matmul(pp[:, 0:QR], ws.Kfm[c][64:128, k:k + 128],
                              q2[64:128, 0:QR], start=True, stop=True,
                              tile_position=(64, 0)),
                     ins=[ws.Kfm[pr], qt], outs=[psB])
                t = kt - 4 * r
                if t > 0:
                    for pp in (psA, psB):
                        p.op("vector", lambda E, p2=pp, t2=t:
                             E.tensor_scalar(p2[:, 0:128 * t2], p2[:, 0:128 * t2],
                                             -1e30, None, ALU.add),
                             ins=[pp], outs=[pp])
                if t >= 0:
                    for pp in (psA, psB):
                        p.op("vector", lambda E, p2=pp, t2=t:
                             E.tensor_add(p2[:, 128 * t2:128 * (t2 + 1)],
                                          p2[:, 128 * t2:128 * (t2 + 1)],
                                          ws.tril128[:]),
                             ins=[pp, ws.tril128], outs=[pp])
                if dbg is not None and r == 0 and pr == 0 and kt == 0:
                    stg = ws.s0t[0]
                    p.op("vector", lambda E, pp=psA, o=stg:
                         E.tensor_copy(o[:], pp[:, 0:QR]), ins=[psA], outs=[stg])
                    tk = p.dma("sync", dbg["s00"], stg[:], ins=[stg],
                               outs=["dbg_s00"], sem="dbgw")
                    p.keep(tk)
                qsc = cst.constf(0.125)
                p.op("scalar", lambda E, pp=psA, o=uA, sc=qsc:
                     E.activation(out=o[:], in_=pp[:, 0:QR], func=AF.Exp, scale=sc[:, 0:1]),
                     ins=[psA, qsc], outs=[uA])
                p.op("scalar", lambda E, pp=psB, o=uB, sc=qsc:
                     E.activation(out=o[:], in_=pp[:, 0:QR], func=AF.Exp, scale=sc[:, 0:1]),
                     ins=[psB, qsc], outs=[uB])
                if dbg is not None and r == 0 and pr == 0 and kt == 0:
                    tk = p.dma("sync", dbg["u00"], uA, ins=[uA],
                               outs=["dbg_u00"], sem="dbgw")
                    p.keep(tk)
                st = kt == 0
                sp = kt == nkt - 1
                p.op("tensor", lambda E, v=vbA, a=uA, st2=st, sp2=sp:
                     E.matmul(psC[0:64, 0:QR], v, a[:],
                              start=st2, stop=sp2, tile_position=(0, 0)),
                     ins=[f"vb{kt}", uA], outs=[psC])
                p.op("tensor", lambda E, v=vbB, a=uB, st2=st, sp2=sp:
                     E.matmul(psC[64:128, 0:QR], v, a[:],
                              start=st2, stop=sp2, tile_position=(0, 64)),
                     ins=[f"vb{kt}", uB], outs=[psC])
                p.op("tensor", lambda E, a=uA, st2=st, sp2=sp:
                     E.matmul(psL[0:1, 0:QR], cst.ones_col_bf[:], a[:],
                              start=st2, stop=sp2, tile_position=(0, 0)),
                     ins=[uA, cst.ones_col_bf], outs=[psL])
                p.op("tensor", lambda E, a=uB, st2=st, sp2=sp:
                     E.matmul(psL[32:33, 0:QR], cst.ones_col_bf[:], a[:],
                              start=st2, stop=sp2, tile_position=(0, 32)),
                     ins=[uB, cst.ones_col_bf], outs=[psL])
                if dbg is not None and r == 0 and pr == 0:
                    stg = ws.s0t[1]
                    p.op("vector", lambda E, o=stg:
                         E.tensor_copy(o[:], psC[:, 0:QR]), ins=[psC], outs=[stg])
                    tk = p.dma("sync", dbg[f"pk{kt}"], stg[:], ins=[stg],
                               outs=[f"dbg_pk{kt}"], sem="dbgw")
                    p.keep(tk)
                    tk = p.dma("sync", dbg[f"uk{kt}"], uA, ins=[uA],
                               outs=[f"dbg_uk{kt}"], sem="dbgw")
                    p.keep(tk)
            ft = ws.ftile64
            p.op("vector", lambda E: E.tensor_scalar(ft, psL[0:64, 0:QR],
                                                     ATT_EPS, None, ALU.add),
                 ins=[psL], outs=[ws.xf4])
            p.op("vector", lambda E: E.reciprocal(ft, ft),
                 ins=[ws.xf4], outs=[ws.xf4])
            if dbg is not None and r == 0 and pr == 0:
                tk = p.dma("sync", dbg["f00"], ft, ins=[ws.xf4],
                           outs=["dbg_f00"], sem="dbgw")
                p.keep(tk)
            p.op("tensor", lambda E: E.matmul(psF[:, 0:QR], ws.sel64[:], ft,
                                              start=True, stop=True),
                 ins=[ws.xf4, ws.sel64], outs=[psF])
            fsb = ws.fsb
            p.op("vector", lambda E: E.tensor_copy(fsb, psF[:, 0:QR]),
                 ins=[psF], outs=[ws.xf5])
            if dbg is not None and r == 0 and pr == 0:
                stg = ws.s0t[1]
                p.op("vector", lambda E, o=stg:
                     E.tensor_copy(o[:], psC[:, 0:QR]), ins=[psC], outs=[stg])
                tk = p.dma("sync", dbg["pc00"], stg[:], ins=[stg],
                           outs=["dbg_pc00"], sem="dbgw")
                p.keep(tk)
                tk = p.dma("sync", dbg["fs00"], ws.xf5[:], ins=[ws.xf5],
                           outs=["dbg_fs00"], sem="dbgw")
                p.keep(tk)
            p.op("vector", lambda E, c=pr: E.tensor_mul(ws.ctx[c][:], psC[:, 0:QR], fsb),
                 ins=[psC, ws.xf5], outs=[ws.ctx[pr]])
            if dbg is not None and r == 0 and pr == 0:
                tk = p.dma("sync", dbg["c00"], ws.ctx[0][:], ins=[ws.ctx[0]],
                           outs=["dbg_c00"], sem="dbgw")
                p.keep(tk)
            yield
        for m in range(NCH):
            wo = ws.wobuf[m % 2]
            p.dma("gpsimd", wo[:], ws.d_woT[m], ins=["woT@hbm"], outs=[wo])
            bs = ws.bcur[m % 2]
            p.dma("gpsimd", bs[:], ws.d_bo[0:1, m * 128:(m + 1) * 128],
                  ins=["bo@hbm"], outs=[bs])
            for kc in range(NCH):
                p.op("tensor", lambda E, w=wo, k2=kc, st=(kc == 0):
                     E.matmul(psO[:, 0:QR], w[:, k2 * 128:(k2 + 1) * 128],
                              ws.ctx[k2][:], start=st, stop=False),
                     ins=[wo, ws.ctx[kc]], outs=[psO])
            p.op("tensor", lambda E, bb=bs:
                 E.matmul(psO[:, 0:QR], bb[:], cst.ones_row_bf[:],
                          start=False, stop=True),
                 ins=[bs, cst.ones_row_bf], outs=[psO])
            so = ws.s0t[m % 2]
            p.op("scalar", lambda E, o=so: E.copy(out=o[:], in_=psO[:, 0:QR]),
                 ins=[psO], outs=[so])
            tok = p.dma("scalar", s0_dram[m * 128:(m + 1) * 128, seq * S + qs:seq * S + qs + QR],
                        so[:], ins=[so], outs=[f"s0@hbm:{qs}:{m}"],
                        sem=f"s0w{qs}p{m % 2}")
            p.keep(tok)
        p.barrier(f"s0all:{qs}:0", f"s0w{qs}p0")
        p.barrier(f"s0all:{qs}:1", f"s0w{qs}p1")
        yield


def gen_l0_ffn(p, ws, cst, seq, x_dram, s0_dram, x1_dram, dbg=None):
    nc = p.nc
    psS, psQ = ws.ps[4], ws.ps[5]
    NT = S // T_FFN
    HH = DH // 2

    def dump(name, tile, key):
        tk = p.dma("sync", dbg[name], tile, ins=[key], outs=["dbg_" + name],
                   sem="dbgw")
        p.keep(tk)
    for tt in range(NT):
        ts = tt * T_FFN
        for m in range(32):
            wa = ws.wsmall[m % 2]
            wb2 = ws.wsmall[2 + m % 2]
            p.dma("sync", wa[:], ws.d_w1T[m], ins=["w1Tbf@hbm"], outs=[wa])
            p.dma("sync", wb2[:], ws.d_w2T[m], ins=["w2Tbf@hbm"], outs=[wb2])
            ps1, ps2 = ws.ps[6], ws.ps[7]
            for kc in range(NCH):
                p.op("tensor", lambda E, w=wa, k2=kc, pp=ps1, t=ts, st=(kc == 0):
                     E.matmul(pp[:, 0:QR], w[:, k2 * 128:(k2 + 1) * 128],
                              ws.xn0[k2][:, t:t + QR], start=st, stop=(k2 == NCH - 1)),
                     ins=[wa, ws.xn0[kc]], outs=[ps1])
            for kc in range(NCH):
                p.op("tensor", lambda E, w=wb2, k2=kc, pp=ps2, t=ts, st=(kc == 0):
                     E.matmul(pp[:, 0:QR], w[:, k2 * 128:(k2 + 1) * 128],
                              ws.xn0[k2][:, t:t + QR], start=st, stop=(k2 == NCH - 1)),
                     ins=[wb2, ws.xn0[kc]], outs=[ps2])
            tsil, tlin = ws.tsil, ws.tlin
            p.op("scalar", lambda E, pp=ps1, mm=m, o=tsil:
                 E.activation(out=o, in_=pp[:, 0:QR], func=AF.Silu,
                              scale=ws.fs1[:, mm:mm + 1]),
                 ins=[ps1, ws.fs1], outs=[ws.x32[2]])
            p.op("scalar", lambda E, pp=ps2, mm=m, o=tlin:
                 E.activation(out=o, in_=pp[:, 0:QR], func=AF.Copy,
                              scale=ws.fs2[:, mm:mm + 1]),
                 ins=[ps2, ws.fs2], outs=[ws.x32[2]])
            p.op("vector", lambda E, mm=m: E.tensor_mul(ws.ha[mm], tsil, tlin),
                 ins=[ws.x32[2]], outs=[ws.habt[m // 4]])
            if dbg is not None and tt == 0 and m == 31:
                for i in range(8):
                    dump(f"hab{i}", ws.habt[i][:], ws.habt[i])
            yield
        for stage, dW, sc_t, src_of, dst_of in (
                (2, ws.d_h1T, ws.fsh1, 0, 8), (3, ws.d_h2T, ws.fsh2, 8, 0)):
            for m in range(32):
                ps1 = ws.ps[6 + m % 2]
                wl = ws.wlarge[m % 2]
                p.dma("sync", wl[:], dW[m], ins=[f"h{stage}Tbf@hbm"], outs=[wl])
                for kc in range(32):
                    p.op("tensor", lambda E, w=wl, k2=kc, pp=ps1, st=(kc == 0),
                         so=src_of:
                         E.matmul(pp[:, 0:QR], w[:, k2 * 128:(k2 + 1) * 128],
                                  ws.hsrc[so + k2 // 4][:, (k2 % 4) * QR:(k2 % 4 + 1) * QR],
                                  start=st, stop=(k2 == 31)),
                         ins=[wl, ws.habt[src_of + kc // 4]], outs=[ps1])
                if dbg is not None and tt == 0 and m == 0 and stage == 2:
                    stg2 = ws.s0t[1]
                    p.op("vector", lambda E, o=stg2, pp=ps1:
                         E.tensor_copy(o[:], pp[:, 0:QR]), ins=[ps1], outs=[stg2])
                    dump("ps2m0", stg2[:], stg2)
                p.op("scalar", lambda E, pp=ps1, mm=m, d2=dst_of, sct=sc_t:
                     E.activation(out=ws.hsrc[d2 + mm // 4][:, (mm % 4) * QR:(mm % 4 + 1) * QR],
                                  in_=pp[:, 0:QR], func=AF.Gelu,
                                  scale=sct[:, mm:mm + 1]),
                     ins=[ps1, sc_t], outs=[ws.habt[dst_of + m // 4]])
                if dbg is not None and tt == 0 and m == 0:
                    dump(f"hs{stage}", ws.hsrc[dst_of][:, 0:QR],
                         ws.habt[dst_of])
                yield
        for m in range(NCH):
            ps1 = ws.ps[6 + m % 2]
            wl = ws.wlarge[m % 2]
            p.dma("sync", wl[:], ws.d_w3T[m], ins=["w3Tbf@hbm"], outs=[wl])
            for kc in range(32):
                p.op("tensor", lambda E, w=wl, k2=kc, pp=ps1, st=(kc == 0):
                     E.matmul(pp[:, 0:QR], w[:, k2 * 128:(k2 + 1) * 128],
                              ws.hsrc[k2 // 4][:, (k2 % 4) * QR:(k2 % 4 + 1) * QR],
                              start=st, stop=(k2 == 31)),
                     ins=[wl, ws.habt[kc // 4]], outs=[ps1])
            g = ws.gtile
            p.op("scalar", lambda E, pp=ps1, mm=m, o=g:
                 E.activation(out=o[:], in_=pp[:, 0:QR], func=AF.Silu,
                              scale=ws.fs3[:, mm:mm + 1]),
                 ins=[ps1, ws.fs3], outs=[g])
            if dbg is not None and tt == 0 and m == 0:
                dump("g0", g[:], g)
            s0i = ws.s0t[m % 2]
            xi = ws.x32[m % 2]
            off = seq * S + ts
            p.dma("sync", s0i[:], s0_dram[m * 128:(m + 1) * 128, off:off + QR],
                  ins=[f"s0all:{ts}:0", f"s0all:{ts}:1", f"s0@hbm:{ts}:{m}"],
                  outs=[s0i])
            p.dma("sync", xi[:], x_dram[m * 128:(m + 1) * 128, off:off + QR],
                  ins=["x@hbm"], outs=[xi])
            p.op("vector", lambda E, a=g, b2=s0i: E.tensor_mul(a[:], a[:], b2[:]),
                 ins=[g, s0i], outs=[g])
            p.op("vector", lambda E, a=g, b2=xi: E.tensor_add(a[:], a[:], b2[:]),
                 ins=[xi, g], outs=[g])
            tok = p.dma("sync", x1_dram[m * 128:(m + 1) * 128, off:off + QR],
                        g[:], ins=[g], outs=[f"x1@hbm:{off}:{m}"], sem="x1w")
            p.keep(tok)
            onesf = cst.constf(1.0)
            sqx = ws.x32[2 + m % 2]
            p.op("vector", lambda E, a=g, o=sqx: E.tensor_mul(o[:], a[:], a[:]),
                 ins=[g], outs=[sqx])
            p.op("tensor", lambda E, a=g, st=(m == 0), sp=(m == NCH - 1):
                 E.matmul(psS[0:1, 0:QR], onesf[:], a[:], start=st, stop=sp),
                 ins=[g, onesf], outs=[psS])
            p.op("tensor", lambda E, a=sqx, st=(m == 0), sp=(m == NCH - 1):
                 E.matmul(psQ[0:1, 0:QR], onesf[:], a[:], start=st, stop=sp),
                 ins=[sqx, onesf], outs=[psQ])
            yield
        RK = "rows"
        mu = ws.rows[0:1, 0:QR]
        var = ws.rows[0:1, QR:2 * QR]
        tmp = ws.rows[0:1, 2 * QR:3 * QR]
        p.op("vector", lambda E: E.tensor_scalar_mul(mu, psS[0:1, 0:QR], 1.0 / D),
             ins=[psS], outs=[RK])
        p.op("vector", lambda E: E.tensor_scalar_mul(var, psQ[0:1, 0:QR], 1.0 / D),
             ins=[psQ], outs=[RK])
        p.op("vector", lambda E: E.tensor_mul(tmp, mu, mu), ins=[RK], outs=[RK])
        p.op("vector", lambda E: E.tensor_sub(var, var, tmp), ins=[RK], outs=[RK])
        p.op("vector", lambda E: E.tensor_scalar_add(tmp, var, LN_EPS),
             ins=[RK], outs=[RK])
        p.op("vector", lambda E: E.reciprocal(tmp, tmp), ins=[RK], outs=[RK])
        p.op("vector", lambda E: E.tensor_mul(tmp, var, tmp), ins=[RK], outs=[RK])
        onet = cst.constf(1.0)
        p.op("scalar", lambda E, b=onet: E.activation(tmp, tmp, AF.Ln, bias=b[0:1, 0:1]),
             ins=[RK, onet], outs=[RK])
        p.op("vector", lambda E, sl=ws.lvsum[0:1, seq * 4 + tt:seq * 4 + tt + 1]:
             E.reduce_sum(sl, tmp, axis=AX.X), ins=[RK], outs=["lvacc"])
        yield


A_INPUT_NAMES = ["wqT", "wkT", "wvT", "woT", "bq", "bk", "bv", "bo", "ropeC",
                 "ropeS", "pswap", "tril128", "sel64", "w1T", "w2T", "h1T",
                 "h2T", "w3T", "fs1", "fs2", "fsh1", "fsh2", "fs3", "bvrow0",
                 "bvrow1"]


def build_ws_A(p, stack, dr):
    nc = p.nc
    e = stack.enter_context
    cst = make_consts(p, stack)
    ws = WS()
    ws.ps = [e(nc.psum_tensor(f"pp{i}", [128, QR], F32)) for i in range(8)]
    p.exclusive.update(t.name for t in ws.ps)
    ws.x32 = [e(nc.sbuf_tensor(f"x32_{i}", [128, QR], F32)) for i in range(5)]
    ws.rows = e(nc.sbuf_tensor("rows", [1, QR * 3], F32))
    ws.ftile64 = ws.x32[4][0:64, 0:QR]
    ws.xf4 = ws.x32[4]
    ws.gtile = ws.x32[4]
    ws.s0t = [e(nc.sbuf_tensor(f"s0t{i}", [128, QR], F32)) for i in range(2)]
    ws.fsb = ws.s0t[1][:]
    ws.xf5 = ws.s0t[1]
    ws.sqf = [ws.s0t[1], ws.s0t[0]]
    ws.rt1 = ws.s0t[0][:]
    ws.rt2 = ws.s0t[1][:]
    ws.tsil = ws.x32[2][:].bitcast(BF16)[:, 0:QR]
    ws.tlin = ws.x32[2][:].bitcast(BF16)[:, QR:2 * QR]
    ws.xn0 = [e(nc.sbuf_tensor(f"xn0_{i}", [128, S], BF16)) for i in range(NCH)]
    ws.scr = [e(nc.sbuf_tensor(f"scr{i}", [128, S], BF16)) for i in range(NCH)]
    ws.Kfm = ws.scr
    ws.qsb = [e(nc.sbuf_tensor(f"qsb{i}", [128, QR], BF16)) for i in range(NCH)]
    import os as _os
    _hw = 4 if _os.environ.get("A_TEST_SMALL_HABT") else 4 * QR
    ws.habt = [e(nc.sbuf_tensor(f"hab{i}", [128, _hw if i < 14 else 4 * QR], BF16))
               for i in range(16)]
    ws.hsrc = ws.habt
    if _hw == 4 * QR:
        ws.ha = [ws.habt[m // 4][:, (m % 4) * QR:(m % 4 + 1) * QR] for m in range(32)]
        ws.hb = [ws.habt[8 + m // 4][:, (m % 4) * QR:(m % 4 + 1) * QR] for m in range(32)]
    ws.ropeC = ws.habt[14][:]
    ws.ropeS = ws.habt[15][:]
    ws.ut = e(nc.sbuf_tensor("ut", [128, 4 * QR], BF16))
    ws.u = [ws.ut[:, i * QR:(i + 1) * QR] for i in range(4)]
    ws.rraw = [ws.u[0], ws.u[1]]
    ws.wvchunk = [ws.x32[i][:].bitcast(BF16) for i in range(5)] +                  [None, ws.ut[:, 0:D], ws.ut[:, D:2 * D]]
    ws.wvk = [ws.x32[i] for i in range(5)] + [None, "ut", "ut"]
    ws.vbuf = [e(nc.sbuf_tensor(f"vbuf{i}", [128, 256], BF16)) for i in range(16)]
    ws.ctx = [e(nc.sbuf_tensor(f"ctx{i}", [128, QR], BF16)) for i in range(NCH)]
    ws.wsmall = [e(nc.sbuf_tensor(f"wsm{i}", [128, D], BF16)) for i in range(4)]
    ws.wstrip = ws.wsmall[:2]
    ws.wobuf = [e(nc.sbuf_tensor(f"wob{i}", [128, D], BF16)) for i in range(2)]
    ws.wvchunk[5] = ws.wobuf[0][:]
    ws.wvk[5] = ws.wobuf[0]
    ws.wlarge = [e(nc.sbuf_tensor(f"wlg{i}", [128, DH], BF16)) for i in range(2)]
    ws.lvsum = e(nc.sbuf_tensor("lvsum", [1, 8], F32))
    _bc = e(nc.sbuf_tensor("bc0", [1, 128], BF16))
    ws.bcur = [_bc, _bc]
    ws.d_bq, ws.d_bk = dr["bq"], dr["bk"]
    ws.d_bv, ws.d_bo = dr["bv"], dr["bo"]
    ws.d_ropeC, ws.d_ropeS = dr["ropeC"], dr["ropeS"]
    ws.d_bvrow_dram = [dr[f"bvrow{f}"] for f in range(2)]
    ws.d_bvrow = [ws.rows[:].bitcast(BF16)[0:1, f * QR:(f + 1) * QR]
                  for f in range(2)]
    ws.pswap = sbuf_const(p, stack, "pswap", dr["pswap"])
    ws.tril128 = sbuf_const(p, stack, "tril128", dr["tril128"], BF16)
    ws.sel64 = sbuf_const(p, stack, "sel64", dr["sel64"])
    ws.fs1 = sbuf_const(p, stack, "fs1", dr["fs1"])
    ws.fs2 = sbuf_const(p, stack, "fs2", dr["fs2"])
    ws.fsh1 = sbuf_const(p, stack, "fsh1", dr["fsh1"])
    ws.fsh2 = sbuf_const(p, stack, "fsh2", dr["fsh2"])
    ws.fs3 = sbuf_const(p, stack, "fs3", dr["fs3"])
    ws.d_wqT, ws.d_wkT, ws.d_wvT = dr["wqT"], dr["wkT"], dr["wvT"]
    ws.d_woT = dr["woT"]
    ws.d_w1T, ws.d_w2T = dr["w1T"], dr["w2T"]
    ws.d_h1T, ws.d_h2T, ws.d_w3T = dr["h1T"], dr["h2T"], dr["w3T"]
    return cst, ws


def build_launch_A(host_meta, ffn_per_attn=2):
    import contextlib
    nc = bass.Bass("TRN2", target_bir_lowering=False)
    p = Prog(nc)
    dr = declare_inputs(nc, host_meta, A_INPUT_NAMES)
    x_dram = nc.dram_tensor("xfm", [D, TOK], F32, kind="ExternalInput").ap()
    x1_dram = nc.dram_tensor("x1fm", [D, TOK], F32, kind="ExternalOutput").ap()
    lv_dram = nc.dram_tensor("lvpart", [1, 1], F32, kind="ExternalOutput").ap()
    s0_dram = nc.dram_tensor("s0fm", [D, TOK], F32).ap()
    v_dram = nc.dram_tensor("v0tm", [S, D], BF16).ap()
    q_dram = nc.dram_tensor("q0fm", [D, S], BF16).ap()
    wbf_shapes = (("w1T", 32, D), ("w2T", 32, D), ("h1T", 32, DH),
                  ("h2T", 32, DH), ("w3T", 8, DH))
    wbf = {}
    for nm, n, inner in wbf_shapes:
        wbf[nm] = nc.dram_tensor(nm + "bf", [n, 128, inner], BF16).ap()

    stack = contextlib.ExitStack()
    with stack:
        cst, ws = build_ws_A(p, stack, dr)
        ws.d_w1T, ws.d_w2T = wbf["w1T"], wbf["w2T"]
        ws.d_h1T, ws.d_h2T, ws.d_w3T = wbf["h1T"], wbf["h2T"], wbf["w3T"]
        for seq in range(SEQ_PER_CORE):
            emit_ln(p, ws, cst, x_dram, seq, xn_bf=ws.xn0, tag=f"ln0s{seq}",
                    src_key="x@hbm")
            emit_l0_qkv(p, ws, cst, seq, v_dram, q_dram)
            if seq == 0:
                for nm, n, inner in wbf_shapes:
                    for m in range(n):
                        p.dma("gpsimd", wbf[nm][m], dr[nm][m], ins=[nm + "@hbm"],
                              outs=[f"{nm}bf:{m}"], sem="wcast")
                for nm, _n, _i in wbf_shapes:
                    p.writer[nm + "bf@hbm"] = ("wcast", p.cnt["wcast"], True,
                                               "gpsimd")
            ag = gen_l0_attn(p, ws, cst, seq, s0_dram, v_dram, q_dram)
            fg = gen_l0_ffn(p, ws, cst, seq, x_dram, s0_dram, x1_dram)
            a_done = f_done = False
            while not (a_done and f_done):
                if not a_done:
                    a_done = next(ag, "END") == "END"
                n_f = ffn_per_attn if not a_done else 10 ** 9
                for _ in range(n_f):
                    if f_done:
                        break
                    f_done = next(fg, "END") == "END"
        lvtot = ws.rows[0:1, 0:1]
        p.op("vector", lambda E: E.reduce_sum(lvtot, ws.lvsum[0:1, :], axis=AX.X),
             ins=["lvacc", ws.lvsum], outs=["rows"])
        tok = p.dma("sync", lv_dram, lvtot, ins=["rows"], outs=["lv@hbm"])
        p.keep(tok)
        p.materialize(stack)
    return nc


B_INPUT_NAMES = ["w1qT", "w2qT", "w1kT", "w2kT", "w1vT", "w2vT",
                 "b1q", "b2q", "b1k", "b2k", "b1v", "b2v", "ln_w", "ln_b",
                 "mask1_0", "mask1_1", "mask1_2", "mask1_3", "qprojT",
                 "routerT", "akT", "auxv", "relm", "uw1T", "uw2T", "ub1",
                 "ub2", "gw1", "gbias", "identf"]
L1_SCALE = float(np.float32(1.0) / np.sqrt(np.float32(D * S)))
GMEAN_N = float(B * S)


def build_ws_B(p, stack, dr):
    nc = p.nc
    e = stack.enter_context
    cst = make_consts(p, stack)
    ws = WS()
    ws.ps = [e(nc.psum_tensor(f"pp{i}", [128, QR], F32)) for i in range(8)]
    p.exclusive.update(t.name for t in ws.ps)
    ws.x32 = [e(nc.sbuf_tensor(f"x32_{i}", [128, QR], F32)) for i in range(NCH)]
    ws.xtmp = e(nc.sbuf_tensor("xtmp", [128, QR], F32))
    ws.rows = e(nc.sbuf_tensor("rows", [1, QR * 4], F32))
    ws.lvB = e(nc.sbuf_tensor("lvB", [1, QR], F32))
    ws.T1 = [e(nc.sbuf_tensor(f"T1_{i}", [128, S], BF16)) for i in range(NCH)]
    ws.QB = [e(nc.sbuf_tensor(f"QB_{i}", [128, S], BF16)) for i in range(NCH)]
    ws.KB = [e(nc.sbuf_tensor(f"KB_{i}", [128, S], BF16)) for i in range(NCH)]
    ws.xbf = [ws.T1[0][:, j * QR:(j + 1) * QR] for j in range(2)]
    ws.sq = [ws.T1[1][:, j * QR:(j + 1) * QR] for j in range(2)]
    ws.varB = [e(nc.sbuf_tensor(f"varB{i}", [128, 2 * DH], BF16)) for i in range(2)]
    ws.Vtm = [ws.varB[i // 8][:, (i % 8) * D:(i % 8 + 1) * D] for i in range(S // 128)]
    ws.upA = e(nc.sbuf_tensor("upA", [128, 2 * DH], BF16))
    ws.uprime = [ws.upA[:, i * QR:(i + 1) * QR] for i in range(16)]
    ws.auxv_tiles = [(ws.T1 + ws.QB)[n // 2][:, (n % 2) * D:(n % 2 + 1) * D]
                     for n in range(32)]
    ws.e_nm = [ws.KB[i // 4][:, (i % 4) * QR:(i % 4 + 1) * QR] for i in range(32)]
    ws.scores_tm = ws.varB[0][:].bitcast(F32)[:, 0:4096]
    ws.rel_bcast = ws.upA[:].bitcast(F32)[:, 0:4096]
    ws.ctx32 = ws.x32
    ws.rkbf = e(nc.sbuf_tensor("rkbf", [48, 4096], BF16))
    ws.qf = ws.x32[5][:].bitcast(BF16)[:, 0:QR]
    ws.rq = e(nc.sbuf_tensor("rq", [48, QR], BF16))
    ws.h1bf = [ws.x32[4][:].bitcast(BF16)[:, j * QR:(j + 1) * QR] for j in range(2)]
    ws.tp8 = e(nc.sbuf_tensor("tp8", [128, 128], F32))
    ws.m1row = ws.rows[0:1, QR:2 * QR]
    ws.t8row = ws.rows[0:1, 2 * QR:3 * QR]
    ws.zrow = ws.rows[0:1, 3 * QR:4 * QR]
    ws.grow = ws.x32[7][0:1, 0:QR]
    ws.m1b = e(nc.sbuf_tensor("m1b", [128, QR], F32))
    ws.d8b = e(nc.sbuf_tensor("d8b", [128, QR], F32))
    ws.geffb = e(nc.sbuf_tensor("geffb", [128, QR], F32))
    ws.sqf = [ws.xtmp, ws.geffb]
    ws.enm32 = ws.xtmp
    ws.enme = ws.x32[0][:].bitcast(BF16)[:, 0:QR]
    ws.enmm = ws.x32[0][:].bitcast(BF16)[:, QR:2 * QR]
    ws.gsc = e(nc.sbuf_tensor("gsc", [1, 8], F32))
    ws.wsm = [e(nc.sbuf_tensor(f"wsmB{i}", [128, D], BF16)) for i in range(4)]
    ws.w2vch = [ws.upA[:, (i * D):(i + 1) * D] for i in range(8)]
    ws.relrow = ws.rows[0:1, QR:2 * QR]
    ws.b1cur = e(nc.sbuf_tensor("b1cur", [1, D], BF16))
    ws.b2cur = e(nc.sbuf_tensor("b2cur", [1, D], BF16))
    ws.d_b1 = {n: dr[f"b1{n}"] for n in "qkv"}
    ws.d_b2 = {n: dr[f"b2{n}"] for n in "qkv"}
    ws.lnw = sbuf_const(p, stack, "ln_w", dr["ln_w"])
    ws.lnb = sbuf_const(p, stack, "ln_b", dr["ln_b"])
    ws.mask1 = [sbuf_const(p, stack, f"mask1_{t}", dr[f"mask1_{t}"], BF16) for t in range(4)]
    ws.routerT = sbuf_const(p, stack, "routerT", dr["routerT"], BF16)
    ws.uw2sb = e(nc.sbuf_tensor("uw2sb", [128, 2], BF16))
    for j in range(2):
        p.dma("gpsimd", ws.uw2sb[:, j:j + 1], dr["uw2T"][j * 128:(j + 1) * 128, :],
              ins=["uw2T@hbm"], outs=[ws.uw2sb])
    ws.ub1 = sbuf_const(p, stack, "ub1", dr["ub1"], BF16)
    ws.ub2 = sbuf_const(p, stack, "ub2", dr["ub2"])
    ws.gw1 = sbuf_const(p, stack, "gw1", dr["gw1"])
    ws.gbias = sbuf_const(p, stack, "gbias", dr["gbias"])
    ws.identf = sbuf_const(p, stack, "identf", dr["identf"])
    ws.d_w1T = {n: dr[f"w1{n}T"] for n in "qkv"}
    ws.d_w2T = {n: dr[f"w2{n}T"] for n in "qkv"}
    ws.d_qprojT = dr["qprojT"]
    ws.d_akT = dr["akT"]
    ws.d_auxv = dr["auxv"]
    ws.d_relm = dr["relm"]
    ws.d_uw1T = dr["uw1T"]
    return cst, ws


def emit_l1_mlp(p, ws, cst, seq, xn1_dram):
    nc = p.nc
    for proj in "qkv":
        b1, b2 = ws.b1cur, ws.b2cur
        p.dma("gpsimd", b1[:], ws.d_b1[proj], ins=[f"b1{proj}@hbm"], outs=[b1])
        p.dma("gpsimd", b2[:], ws.d_b2[proj], ins=[f"b2{proj}@hbm"], outs=[b2])
        for r in range(NQR):
            ts = seq * S + r * QR
            xnb = [ws.x32[kc // 2][:].bitcast(BF16)[:, (kc % 2) * QR:(kc % 2 + 1) * QR]
                   for kc in range(NCH)]
            for kc in range(NCH):
                p.dma("gpsimd", xnb[kc],
                      xn1_dram[kc * 128:(kc + 1) * 128, ts:ts + QR],
                      ins=["xn1@hbm"], outs=[ws.x32[kc // 2]])
            for m in range(NCH):
                wb = ws.wsm[m % 2]
                p.dma("gpsimd", wb[:], ws.d_w1T[proj][m],
                      ins=[f"w1{proj}@hbm"], outs=[wb])
                ps = ws.ps[m % 2]
                for kc in range(NCH):
                    p.op("tensor", lambda E, w=wb, k2=kc, x=xnb[kc], pp=ps, st=(kc == 0):
                         E.matmul(pp[:, 0:QR], w[:, k2 * 128:(k2 + 1) * 128],
                                  x, start=st, stop=False),
                         ins=[wb, ws.x32[kc // 2]], outs=[ps])
                p.op("tensor", lambda E, bb=b1, mm=m, pp=ps:
                     E.matmul(pp[:, 0:QR], bb[0:1, mm * 128:(mm + 1) * 128],
                              cst.ones_row_bf[:], start=False, stop=True),
                     ins=[b1, cst.ones_row_bf], outs=[ps])
                p.op("scalar", lambda E, pp=ps, mm=m, rr=r:
                     E.activation(out=ws.T1[mm][:, rr * QR:(rr + 1) * QR],
                                  in_=pp[:, 0:QR], func=AF.Gelu),
                     ins=[ps], outs=[ws.T1[m]])
        if proj in "qk":
            dst = ws.QB if proj == "q" else ws.KB
            for r in range(NQR):
                for m in range(NCH):
                    wb = ws.wsm[2 + m % 2]
                    p.dma("gpsimd", wb[:], ws.d_w2T[proj][m],
                          ins=[f"w2{proj}@hbm"], outs=[wb])
                    ps = ws.ps[2 + m % 2]
                    for kc in range(NCH):
                        p.op("tensor", lambda E, w=wb, k2=kc, rr=r, pp=ps, st=(kc == 0):
                             E.matmul(pp[:, 0:QR], w[:, k2 * 128:(k2 + 1) * 128],
                                      ws.T1[k2][:, rr * QR:(rr + 1) * QR],
                                      start=st, stop=False),
                             ins=[wb, ws.T1[kc]], outs=[ps])
                    p.op("tensor", lambda E, bb=b2, mm=m, pp=ps:
                         E.matmul(pp[:, 0:QR], bb[0:1, mm * 128:(mm + 1) * 128],
                                  cst.ones_row_bf[:], start=False, stop=True),
                         ins=[b2, cst.ones_row_bf], outs=[ps])
                    p.op("scalar", lambda E, pp=ps, mm=m, rr=r, dd=dst[m]:
                         E.activation(out=dd[:, rr * QR:(rr + 1) * QR],
                                      in_=pp[:, 0:QR], func=AF.Copy),
                         ins=[ps], outs=[dst[m]])
        else:
            for kc in range(NCH):
                p.dma("gpsimd", ws.w2vch[kc], ws.d_w2T["v"][kc * 128:(kc + 1) * 128, :],
                      ins=["w2v@hbm"], outs=[ws.upA])
            for tt in range(S // 128):
                ts = tt * 128
                for fh in range(2):
                    ps = ws.ps[4 + fh]
                    for kc in range(NCH):
                        p.op("tensor", lambda E, k2=kc, t=ts, f=fh, pp=ps, st=(kc == 0):
                             E.matmul(pp[:, 0:QR], ws.T1[k2][:, t:t + 128],
                                      ws.w2vch[k2][:, f * QR:(f + 1) * QR],
                                      start=st, stop=False),
                             ins=[ws.T1[kc], ws.upA], outs=[ps])
                    p.op("tensor", lambda E, f=fh, pp=ps:
                         E.matmul(pp[:, 0:QR], cst.ones_row_bf[0:1, 0:128],
                                  b2[0:1, f * QR:(f + 1) * QR], start=False, stop=True),
                         ins=[b2, cst.ones_row_bf], outs=[ps])
                    p.op("scalar", lambda E, pp=ps, dd=ws.Vtm[tt], f=fh:
                         E.copy(out=dd[:, f * QR:(f + 1) * QR], in_=pp[:, 0:QR]),
                         ins=[ps], outs=[ws.varB[tt // 8]])


def emit_l1_attn(p, ws, cst, seq, s1_dram):
    nc = p.nc
    sc_t = cst.constf(L1_SCALE)
    for r in range(NQR):
        qs = r * QR
        nkt = (r + 1) * 4
        for kt in range(nkt):
            ps = ws.ps[kt % 2]
            ks = kt * 128
            for kc in range(NCH):
                p.op("tensor", lambda E, k2=kc, k=ks, q=qs, pp=ps, st=(kc == 0):
                     E.matmul(pp[:, 0:QR], ws.KB[k2][:, k:k + 128],
                              ws.QB[k2][:, q:q + QR], start=st, stop=(k2 == NCH - 1)),
                     ins=[ws.KB[kc], ws.QB[kc]], outs=[ps])
            t = kt - 4 * r
            if t >= 0:
                p.op("vector", lambda E, pp=ps, m=ws.mask1[t], u=ws.uprime[kt]:
                     E.tensor_mul(u, pp[:, 0:QR], m[:]),
                     ins=[ps, ws.mask1[t]], outs=[ws.upA])
            else:
                p.op("scalar", lambda E, pp=ps, u=ws.uprime[kt], sc=sc_t:
                     E.activation(out=u, in_=pp[:, 0:QR], func=AF.Copy,
                                  scale=sc[:, 0:1]),
                     ins=[ps, sc_t], outs=[ws.upA])
        for hc in range(NCH):
            ps = ws.ps[4 + hc % 2]
            for kt in range(nkt):
                p.op("tensor", lambda E, k2=kt, h=hc, pp=ps, st=(kt == 0):
                     E.matmul(pp[:, 0:QR], ws.Vtm[k2][:, h * 128:(h + 1) * 128],
                              ws.uprime[k2], start=st, stop=(k2 == nkt - 1)),
                     ins=[ws.varB[kt // 8], ws.upA], outs=[ps])
            p.op("vector", lambda E, pp=ps, o=ws.ctx32[hc]:
                 E.tensor_copy(o[:], pp[:, 0:QR]), ins=[ps], outs=[ws.ctx32[hc]])
        RK = "rows"
        mu = ws.rows[0:1, 0:QR]
        var = ws.rows[0:1, QR:2 * QR]
        rstd = ws.rows[0:1, 2 * QR:3 * QR]
        tmp = ws.rows[0:1, 3 * QR:4 * QR]
        ps_s, ps_q = ws.ps[6], ws.ps[7]
        onesf = cst.constf(1.0)
        for hc in range(NCH):
            sqt = ws.xtmp
            p.op("vector", lambda E, a=ws.ctx32[hc], o=sqt: E.tensor_mul(o[:], a[:], a[:]),
                 ins=[ws.ctx32[hc]], outs=[sqt])
            p.op("tensor", lambda E, a=ws.ctx32[hc], st=(hc == 0), sp=(hc == NCH - 1):
                 E.matmul(ps_s[0:1, 0:QR], onesf[:], a[:], start=st, stop=sp),
                 ins=[ws.ctx32[hc], onesf], outs=[ps_s])
            p.op("tensor", lambda E, a=sqt, st=(hc == 0), sp=(hc == NCH - 1):
                 E.matmul(ps_q[0:1, 0:QR], onesf[:], a[:], start=st, stop=sp),
                 ins=[sqt, onesf], outs=[ps_q])
        p.op("vector", lambda E: E.tensor_scalar_mul(mu, ps_s[0:1, 0:QR], 1.0 / D),
             ins=[ps_s], outs=[RK])
        p.op("vector", lambda E: E.tensor_scalar_mul(var, ps_q[0:1, 0:QR], 1.0 / D),
             ins=[ps_q], outs=[RK])
        p.op("vector", lambda E: E.tensor_mul(tmp, mu, mu), ins=[RK], outs=[RK])
        p.op("vector", lambda E: E.tensor_sub(var, var, tmp), ins=[RK], outs=[RK])
        eps_t = cst.constf(LN_EPS)
        mhalf = cst.constf(-0.5)
        p.op("scalar", lambda E, b=eps_t: E.activation(tmp, var, AF.Ln, bias=b[0:1, 0:1]),
             ins=[RK, eps_t], outs=[RK])
        p.op("scalar", lambda E, s2=mhalf: E.activation(rstd, tmp, AF.Exp, scale=s2[0:1, 0:1]),
             ins=[RK, mhalf], outs=[RK])
        ps_mu, ps_r = ws.ps[0], ws.ps[1]
        p.op("tensor", lambda E: E.matmul(ps_mu[:, 0:QR], cst.ones_row_f[0:1, :],
                                          mu, start=True, stop=True),
             ins=[RK, cst.ones_row_f], outs=[ps_mu])
        p.op("tensor", lambda E: E.matmul(ps_r[:, 0:QR], cst.ones_row_f[0:1, :],
                                          rstd, start=True, stop=True),
             ins=[RK, cst.ones_row_f], outs=[ps_r])
        for hc in range(NCH):
            a = ws.ctx32[hc]
            p.op("vector", lambda E, t2=a: E.tensor_sub(t2[:], t2[:], ps_mu[:, 0:QR]),
                 ins=[a, ps_mu], outs=[a])
            p.op("vector", lambda E, t2=a: E.tensor_mul(t2[:], t2[:], ps_r[:, 0:QR]),
                 ins=[a, ps_r], outs=[a])
            p.op("vector", lambda E, t2=a, h=hc: E.tensor_scalar(
                t2[:], t2[:], ws.lnw[:, h:h + 1], ws.lnb[:, h:h + 1],
                ALU.mult, ALU.add),
                 ins=[a, ws.lnw, ws.lnb], outs=[a])
            tok = p.dma("sync", s1_dram[hc * 128:(hc + 1) * 128,
                                        seq * S + qs:seq * S + qs + QR],
                        a[:], ins=[a], outs=[f"s1@hbm:{seq}:{qs}:{hc}"], sem="s1w")
            p.keep(tok)


def emit_l1_ffn_prep(p, ws, cst):
    nc = p.nc
    for ns in range(8):
        ak = ws.wsm[ns % 2]
        p.dma("gpsimd", ak[:, 0:QR], ws.d_akT[:, ns * QR:(ns + 1) * QR],
              ins=["akT@hbm"], outs=[ak])
        ps = ws.ps[ns % 2]
        p.op("tensor", lambda E, a=ak, pp=ps:
             E.matmul(pp[0:48, 0:QR], ws.routerT[:], a[:, 0:QR],
                      start=True, stop=True),
             ins=[ak, ws.routerT], outs=[ps])
        p.op("scalar", lambda E, pp=ps, n=ns:
             E.copy(out=ws.rkbf[:, n * QR:(n + 1) * QR], in_=pp[0:48, 0:QR]),
             ins=[ps], outs=[ws.rkbf])
    for ns in range(8):
        p.dma("sync", ws.relrow, ws.d_relm[0:1, ns * QR:(ns + 1) * QR],
              ins=["relm@hbm"], outs=["rows"])
        ps = ws.ps[2 + ns % 2]
        p.op("tensor", lambda E, pp=ps:
             E.matmul(pp[:, 0:QR], cst.ones_row_f[0:1, :], ws.relrow,
                      start=True, stop=True),
             ins=["rows", cst.ones_row_f], outs=[ps])
        p.op("vector", lambda E, pp=ps, n=ns:
             E.tensor_copy(ws.rel_bcast[:, n * QR:(n + 1) * QR], pp[:, 0:QR]),
             ins=[ps], outs=[ws.upA])
    g = ws.gsc
    p.op("vector", lambda E: E.reduce_sum(g[0:1, 0:1], ws.gparts_sb[0:1, :], axis=AX.X),
         ins=[ws.gparts_sb], outs=[g])
    p.op("vector", lambda E: E.tensor_scalar_mul(g[0:1, 1:2], g[0:1, 0:1], 1.0 / GMEAN_N),
         ins=[g], outs=[g])
    p.op("vector", lambda E: E.tensor_scalar_add(g[0:1, 1:2], g[0:1, 1:2], 1e-6),
         ins=[g], outs=[g])
    p.op("vector", lambda E: E.reciprocal(g[0:1, 2:3], g[0:1, 1:2]),
         ins=[g], outs=[g])
    p.op("vector", lambda E: E.tensor_scalar_mul(g[0:1, 3:4], g[0:1, 2:3], 0.5),
         ins=[g], outs=[g])
    for n in range(32):
        p.dma("gpsimd", ws.auxv_tiles[n], ws.d_auxv[n],
              ins=["auxv@hbm"], outs=[(ws.T1 + ws.QB)[n // 2]])


def emit_l1_ffn(p, ws, cst, seq, xn1_dram, x1_dram, s1_dram, lv_dram, x2_dram):
    nc = p.nc
    for r in range(NQR):
        ts = seq * S + r * QR
        xnb = [ws.x32[kc // 2][:].bitcast(BF16)[:, (kc % 2) * QR:(kc % 2 + 1) * QR]
               for kc in range(NCH)]
        for kc in range(NCH):
            p.dma("gpsimd", xnb[kc], xn1_dram[kc * 128:(kc + 1) * 128, ts:ts + QR],
                  ins=["xn1@hbm"], outs=[ws.x32[kc // 2]])
        ps = ws.ps[0]
        wqp = ws.wsm[0]
        p.dma("gpsimd", wqp[:], ws.d_qprojT[0], ins=["qproj@hbm"], outs=[wqp])
        for kc in range(NCH):
            p.op("tensor", lambda E, k2=kc, x=xnb[kc], pp=ps, w=wqp, st=(kc == 0):
                 E.matmul(pp[:, 0:QR], w[:, k2 * 128:(k2 + 1) * 128], x,
                          start=st, stop=(k2 == NCH - 1)),
                 ins=[wqp, ws.x32[kc // 2]], outs=[ps])
        p.op("scalar", lambda E, pp=ps: E.copy(out=ws.qf, in_=pp[:, 0:QR]),
             ins=[ps], outs=[ws.x32[5]])
        ps = ws.ps[1]
        p.op("tensor", lambda E, pp=ps:
             E.matmul(pp[0:48, 0:QR], ws.routerT[:], ws.qf, start=True, stop=True),
             ins=[ws.routerT, ws.x32[5]], outs=[ps])
        rdsc = cst.constf(RD_SCALE)
        p.op("scalar", lambda E, pp=ps, sc=rdsc:
             E.activation(out=ws.rq[:], in_=pp[0:48, 0:QR], func=AF.Copy,
                          scale=sc[0:48, 0:1]),
             ins=[ps, rdsc], outs=[ws.rq])
        for m in range(2):
            ps = ws.ps[2 + m]
            wb = ws.wsm[2]
            p.dma("gpsimd", wb[:], ws.d_uw1T[m], ins=["uw1@hbm"], outs=[wb])
            for kc in range(NCH):
                p.op("tensor", lambda E, k2=kc, x=xnb[kc], pp=ps, w=wb, st=(kc == 0):
                     E.matmul(pp[:, 0:QR], w[:, k2 * 128:(k2 + 1) * 128], x,
                              start=st, stop=False),
                     ins=[wb, ws.x32[kc // 2]], outs=[ps])
            p.op("tensor", lambda E, pp=ps, mm=m:
                 E.matmul(pp[:, 0:QR], ws.ub1[0:1, mm * 128:(mm + 1) * 128],
                          cst.ones_row_bf[:], start=False, stop=True),
                 ins=[ws.ub1, cst.ones_row_bf], outs=[ps])
            p.op("scalar", lambda E, pp=ps, o=ws.h1bf[m]:
                 E.activation(out=o, in_=pp[:, 0:QR], func=AF.Gelu),
                 ins=[ps], outs=[ws.x32[4]])
        ps = ws.ps[4]
        for m in range(2):
            p.op("tensor", lambda E, mm=m, pp=ps, st=(m == 0):
                 E.matmul(pp[0:1, 0:QR], ws.uw2sb[:, mm:mm + 1], ws.h1bf[mm],
                          start=st, stop=(mm == 1)),
                 ins=[ws.uw2sb, ws.x32[4]], outs=[ps])
        sig = ws.grow
        p.op("scalar", lambda E, pp=ps:
             E.activation(out=sig, in_=pp[0:1, 0:QR], func=AF.Sigmoid,
                          bias=ws.ub2[0:1, 0:1]),
             ins=[ps, ws.ub2], outs=[ws.x32[7]])
        for tt in range(4):
            tq = tt * 128
            for ns in range(8):
                ps = ws.ps[5 + ns % 2]
                p.op("tensor", lambda E, n=ns, t2=tq, pp=ps:
                     E.matmul(pp[:, 0:QR], ws.rq[:, t2:t2 + 128],
                              ws.rkbf[:, n * QR:(n + 1) * QR], start=True, stop=True),
                     ins=[ws.rq, ws.rkbf], outs=[ps])
                p.op("vector", lambda E, n=ns, pp=ps:
                     E.tensor_add(ws.scores_tm[:, n * QR:(n + 1) * QR], pp[:, 0:QR],
                                  ws.rel_bcast[:, n * QR:(n + 1) * QR]),
                     ins=[ps, ws.upA], outs=[ws.varB[0]])
            p.op("vector", lambda E: E.max(out=ws.tp8[:, 0:8], in_=ws.scores_tm[:]),
                 ins=[ws.varB[0]], outs=[ws.tp8])
            p.op("vector", lambda E: E.tensor_scalar_mul(ws.tp8[:, 8:9], ws.tp8[:, 0:1], -1.0),
                 ins=[ws.tp8], outs=[ws.tp8])
            p.op("scalar", lambda E: E.activation(out=ws.tp8[:, 16:24], in_=ws.tp8[:, 0:8],
                                                  func=AF.Exp, bias=ws.tp8[:, 8:9]),
                 ins=[ws.tp8], outs=[ws.tp8])
            p.op("vector", lambda E: E.reduce_sum(ws.tp8[:, 24:25], ws.tp8[:, 16:24], axis=AX.X),
                 ins=[ws.tp8], outs=[ws.tp8])
            p.op("vector", lambda E: E.tensor_copy(ws.tp8[:, 32:33], ws.tp8[:, 7:8]),
                 ins=[ws.tp8], outs=[ws.tp8])
            p.op("vector", lambda E: E.reciprocal(ws.tp8[:, 64:65], ws.tp8[:, 24:25]),
                 ins=[ws.tp8], outs=[ws.tp8])
            psT = ws.ps[7]
            p.op("tensor", lambda E, pp=psT:
                 E.transpose(pp[:, 0:128], ws.tp8[:], ws.identf[:]),
                 ins=[ws.tp8, ws.identf], outs=[psT])
            for col, row in ((0, ws.m1row), (32, ws.t8row), (64, ws.zrow)):
                p.op("vector", lambda E, c2=col, pp=psT, rw=row, t2=tq:
                     E.tensor_copy(rw[0:1, t2:t2 + 128], pp[c2:c2 + 1, 0:128]),
                     ins=[psT], outs=["rows"])
        p.op("vector", lambda E: E.tensor_sub(ws.t8row, ws.t8row, ws.m1row),
             ins=["rows"], outs=["rows"])
        ps = ws.ps[6]
        p.op("tensor", lambda E, pp=ps:
             E.matmul(pp[:, 0:QR], cst.ones_row_f[0:1, :], ws.m1row,
                      start=True, stop=True),
             ins=["rows", cst.ones_row_f], outs=[ps])
        p.op("vector", lambda E, pp=ps: E.tensor_copy(ws.m1b[:], pp[:, 0:QR]),
             ins=[ps], outs=[ws.m1b])
        p.op("tensor", lambda E, pp=ps:
             E.matmul(pp[:, 0:QR], cst.ones_row_f[0:1, :], ws.t8row,
                      start=True, stop=True),
             ins=["rows", cst.ones_row_f], outs=[ps])
        p.op("vector", lambda E, pp=ps: E.tensor_copy(ws.d8b[:], pp[:, 0:QR]),
             ins=[ps], outs=[ws.d8b])
        for n in range(32):
            ps = ws.ps[n % 2]
            p.op("tensor", lambda E, n2=n, pp=ps:
                 E.matmul(pp[:, 0:QR], ws.rkbf[:, n2 * 128:(n2 + 1) * 128],
                          ws.rq[:], start=True, stop=True),
                 ins=[ws.rkbf, ws.rq], outs=[ps])
            p.op("vector", lambda E, n2=n, pp=ps:
                 E.tensor_scalar(ws.enm32[:], pp[:, 0:QR],
                                 ws.relnm[:, n2:n2 + 1], None, ALU.add),
                 ins=[ps, ws.relnm], outs=[ws.xtmp])
            p.op("vector", lambda E: E.tensor_sub(ws.enm32[:], ws.enm32[:], ws.m1b[:]),
                 ins=[ws.xtmp, ws.m1b], outs=[ws.xtmp])
            p.op("scalar", lambda E: E.activation(out=ws.enme, in_=ws.enm32[:],
                                                  func=AF.Exp),
                 ins=[ws.xtmp], outs=[ws.x32[0]])
            p.op("vector", lambda E: E.tensor_tensor(ws.enmm, ws.enm32[:],
                                                     ws.d8b[:], ALU.is_ge),
                 ins=[ws.xtmp, ws.d8b], outs=[ws.x32[0]])
            p.op("gpsimd", lambda E, n2=n: E.tensor_mul(
                ws.e_nm[n2], ws.enme, ws.enmm),
                 ins=[ws.x32[0]], outs=[ws.KB[n // 4]])
        lvr = ws.rows[0:1, 0:QR]
        p.dma("sync", lvr, lv_dram[0:1, ts:ts + QR], ins=["lv@hbm"], outs=["rows"])
        p.op("vector", lambda E: E.tensor_scalar(lvr, lvr, ws.gsc[0:1, 3:4], None,
                                                 ALU.mult),
             ins=["rows", ws.gsc], outs=["rows"])
        p.op("vector", lambda E: E.tensor_scalar_mul(ws.grow, ws.grow, 2.5),
             ins=[ws.x32[7]], outs=[ws.x32[7]])
        p.op("vector", lambda E: E.tensor_add(lvr, lvr, ws.grow),
             ins=["rows", ws.x32[7]], outs=["rows"])
        p.op("vector", lambda E: E.tensor_scalar_min(lvr, lvr, 5.0),
             ins=["rows"], outs=["rows"])
        p.op("vector", lambda E: E.tensor_scalar_max(lvr, lvr, 0.0),
             ins=["rows"], outs=["rows"])
        p.op("scalar", lambda E: E.activation(out=lvr, in_=lvr, func=AF.Sigmoid,
                                              scale=ws.gw1[0:1, 0:1],
                                              bias=ws.gbias[0:1, 0:1]),
             ins=["rows", ws.gw1, ws.gbias], outs=["rows"])
        p.op("vector", lambda E: E.tensor_mul(lvr, lvr, ws.zrow),
             ins=["rows"], outs=["rows"])
        ps = ws.ps[6]
        p.op("tensor", lambda E, pp=ps:
             E.matmul(pp[:, 0:QR], cst.ones_row_f[0:1, :], lvr,
                      start=True, stop=True),
             ins=["rows", cst.ones_row_f], outs=[ps])
        p.op("vector", lambda E, pp=ps: E.tensor_copy(ws.geffb[:], pp[:, 0:QR]),
             ins=[ps], outs=[ws.geffb])
        for hc in range(NCH):
            ps = ws.ps[2 + hc % 2]
            for n in range(32):
                p.op("tensor", lambda E, n2=n, h=hc, pp=ps, st=(n == 0):
                     E.matmul(pp[:, 0:QR], ws.auxv_tiles[n2][:, h * 128:(h + 1) * 128],
                              ws.e_nm[n2], start=st, stop=(n2 == 31)),
                     ins=[(ws.T1 + ws.QB)[n // 2], ws.KB[n // 4]], outs=[ps])
            a = ws.x32[4 + hc % 2]
            b2t = ws.x32[6 + hc % 2]
            p.op("vector", lambda E, pp=ps, o=a: E.tensor_mul(o[:], pp[:, 0:QR], ws.geffb[:]),
                 ins=[ps, ws.geffb], outs=[a])
            p.dma("sync", b2t[:], xn1_dram[hc * 128:(hc + 1) * 128, ts:ts + QR],
                  ins=["xn1@hbm"], outs=[b2t])
            p.op("vector", lambda E, o=a, x=b2t: E.tensor_add(o[:], o[:], x[:]),
                 ins=[a, b2t], outs=[a])
            p.op("scalar", lambda E, o=a: E.activation(out=o[:], in_=o[:], func=AF.Silu),
                 ins=[a], outs=[a])
            p.dma("sync", b2t[:], s1_dram[hc * 128:(hc + 1) * 128, ts:ts + QR],
                  ins=["s1@hbm"], outs=[b2t])
            p.op("vector", lambda E, o=a, x=b2t: E.tensor_mul(o[:], o[:], x[:]),
                 ins=[a, b2t], outs=[a])
            p.dma("sync", b2t[:], x1_dram[hc * 128:(hc + 1) * 128, ts:ts + QR],
                  ins=["x1@hbm"], outs=[b2t])
            p.op("vector", lambda E, o=a, x=b2t: E.tensor_add(o[:], o[:], x[:]),
                 ins=[a, b2t], outs=[a])
            tok = p.dma("sync", x2_dram[hc * 128:(hc + 1) * 128, ts:ts + QR],
                        a[:], ins=[a], outs=[f"x2@hbm:{ts}:{hc}"], sem="x2w")
            p.keep(tok)


def build_launch_B(host_meta, phases=("ln", "mlp", "attn", "prep", "ffn"),
                   debug=False):
    import contextlib
    nc = bass.Bass("TRN2", target_bir_lowering=False)
    p = Prog(nc)
    dr = declare_inputs(nc, host_meta, B_INPUT_NAMES)
    kind = "ExternalOutput" if debug else "Internal"
    x1_dram = nc.dram_tensor("x1fm", [D, TOK], F32, kind="ExternalInput").ap()
    gparts = nc.dram_tensor("gparts", [1, NCORES], F32, kind="ExternalInput").ap()
    x2_dram = nc.dram_tensor("x2fm", [D, TOK], F32, kind="ExternalOutput").ap()
    xn1_dram = nc.dram_tensor("xn1fm", [D, TOK], F32, kind=kind).ap()
    s1_dram = nc.dram_tensor("s1fm", [D, TOK], F32, kind=kind).ap()
    lv_dram = nc.dram_tensor("lvrows", [1, TOK], F32, kind=kind).ap()

    stack = contextlib.ExitStack()
    with stack:
        e = stack.enter_context
        cst, ws = build_ws_B(p, stack, dr)
        ws.gparts_sb = e(nc.sbuf_tensor("gparts_sb", [1, NCORES], F32))
        p.dma("sync", ws.gparts_sb[:], gparts, ins=["gparts@hbm"],
              outs=[ws.gparts_sb])
        ws.relnm = e(nc.sbuf_tensor("relnm", [128, 32], F32))
        p.dma("sync", ws.relnm[:],
              dr["relm"].rearrange("o (n p) -> p (o n)", p=128),
              ins=["relm@hbm"], outs=[ws.relnm],
              allow_slow_non_contiguous=True)
        if "ln" in phases:
            for seq in range(SEQ_PER_CORE):
                emit_ln(p, ws, cst, x1_dram, seq, xn_bf=None, xn32_dram=xn1_dram,
                        lv_acc=ws.lvB, tag=f"ln1s{seq}", src_key="x1@hbm",
                        lv_dram=lv_dram)
            p.barrier("xn1@hbm", "xnw")
            p.barrier("lv@hbm", "lvw")
        for seq in range(SEQ_PER_CORE):
            if "mlp" in phases:
                emit_l1_mlp(p, ws, cst, seq, xn1_dram)
            if "attn" in phases:
                emit_l1_attn(p, ws, cst, seq, s1_dram)
        if "attn" in phases:
            p.barrier("s1@hbm", "s1w")
        if "prep" in phases:
            emit_l1_ffn_prep(p, ws, cst)
        if "ffn" in phases:
            for seq in range(SEQ_PER_CORE):
                emit_l1_ffn(p, ws, cst, seq, xn1_dram, x1_dram, s1_dram, lv_dram,
                            x2_dram)
        p.materialize(stack)
    return nc


_host_cache = {}


def kernel(**inputs):
    meta = host_prepare(inputs)
    x = np.asarray(inputs["L0_attn_x"], np.float32)

    in_maps = []
    for c in range(NCORES):
        m = {k: v for k, v in meta.items()}
        xc = np.ascontiguousarray(
            x[c * SEQ_PER_CORE:(c + 1) * SEQ_PER_CORE].reshape(TOK, D).T)
        m["xfm"] = xc
        in_maps.append(m)

    ncA = build_launch_A(meta)
    resA = run_bass_kernel_spmd(ncA, in_maps, core_ids=list(range(NCORES)))
    parts = np.stack([np.asarray(resA.results[c]["lvpart"]).reshape(1)
                      for c in range(NCORES)], axis=1)

    in_mapsB = []
    for c in range(NCORES):
        m = {k: v for k, v in meta.items()}
        m["x1fm"] = np.asarray(resA.results[c]["x1fm"])
        m["gparts"] = np.ascontiguousarray(parts)
        in_mapsB.append(m)

    ncB = build_launch_B(meta)
    resB = run_bass_kernel_spmd(ncB, in_mapsB, core_ids=list(range(NCORES)))

    out = np.empty((B, S, D), np.float32)
    for c in range(NCORES):
        x2 = np.asarray(resB.results[c]["x2fm"])
        out[c * SEQ_PER_CORE:(c + 1) * SEQ_PER_CORE] = \
            x2.T.reshape(SEQ_PER_CORE, S, D)
    return out


# version 09 (final)
# speedup vs baseline: 1.1107x; 1.1107x over previous
import numpy as np
import ml_dtypes
import concourse.bass as bass
import concourse.mybir as mybir
from concourse.bass_utils import run_bass_kernel_spmd

F32 = mybir.dt.float32
BF16 = mybir.dt.bfloat16
AF = mybir.ActivationFunctionType
ALU = mybir.AluOpType
AX = mybir.AxisListType
BF16NP = ml_dtypes.bfloat16

B, S, D = 16, 2048, 1024
NCORES = 8
SEQ_PER_CORE = B // NCORES
TOK = SEQ_PER_CORE * S
NH, DK = 16, 64
DH = 4096
NCH = D // 128
QR = 512
NQR = S // QR
LN_EPS = 1e-5
ATT_EPS = 1e-8
RD_SCALE = 1.0 / float(np.sqrt(np.float32(48.0)))
T_FFN = QR
PRECAST = True
LVDBG = False

_ENGINES = ("tensor", "vector", "scalar", "gpsimd", "sync")


class Prog:
    def __init__(self, nc):
        self.nc = nc
        self.queue = {e: [] for e in _ENGINES}
        self.cnt = {}
        self.sem_names = []
        self.writer = {}
        self.readers = {}
        self.satisfied = {}
        self.final_tokens = []
        self.exclusive = set()

    def _sem(self, name):
        if name not in self.cnt:
            self.cnt[name] = 0
            self.sem_names.append(name)
        return name

    @staticmethod
    def _key(r):
        if isinstance(r, str):
            return r
        n = getattr(r, "name", None)
        if n is None:
            raise TypeError(f"resource {r!r} has no name")
        return n

    def _waits_for(self, eng, ins, outs):
        need = {}

        def add(tok):
            if tok is None:
                return
            sem, val, is_dma, peng = tok
            if not is_dma and peng == eng and eng == "tensor":
                return
            if self.satisfied.get((eng, sem), 0) >= val:
                return
            if need.get(sem, 0) < val:
                need[sem] = val

        for r in ins:
            k = self._key(r)
            add(self.writer.get(k))
            if k in self.exclusive:
                for tok in self.readers.get(k, ()):
                    add(tok)
        for r in outs:
            k = self._key(r)
            add(self.writer.get(k))
            for tok in self.readers.get(k, ()):
                add(tok)
        for sem, val in need.items():
            self.satisfied[(eng, sem)] = val
        return list(need.items())

    def _record(self, tok, ins, outs):
        for r in ins:
            self.readers.setdefault(self._key(r), []).append(tok)
        for r in outs:
            k = self._key(r)
            self.writer[k] = tok
            self.readers[k] = []

    def op(self, eng, fn, ins=(), outs=(), sem=None):
        sem = self._sem(sem or ("s_" + eng))
        waits = self._waits_for(eng, ins, outs)
        self.cnt[sem] += 1
        tok = (sem, self.cnt[sem], False, eng)
        self._record(tok, ins, outs)
        self.queue[eng].append((waits, fn, sem, 1))
        return tok

    def dma(self, eng, out_ap, in_ap, ins=(), outs=(), sem=None, **kw):
        if sem is None:
            sem = "gd_" + self._key(outs[0] if outs else "misc")
        sem = self._sem(sem)
        waits = self._waits_for(eng, ins, outs)
        self.cnt[sem] += 16
        tok = (sem, self.cnt[sem], True, eng)
        self._record(tok, ins, outs)
        fn = (lambda E, o=out_ap, i=in_ap, k=dict(kw):
              E.dma_start(out=o, in_=i, **k))
        self.queue[eng].append((waits, fn, sem, 16))
        return tok

    def keep(self, tok):
        self.final_tokens.append(tok)

    def barrier(self, res_key, sem_name, eng="sync"):
        self.writer[res_key] = (self._sem(sem_name), self.cnt.get(sem_name, 0),
                                True, eng)

    def materialize(self, stack):
        nc = self.nc
        sems = {n: stack.enter_context(nc.semaphore(n)) for n in self.sem_names}
        block = stack.enter_context(nc.Block())
        fin = {}
        for sem, val, _d, _e in self.final_tokens:
            fin[sem] = max(fin.get(sem, 0), val)
        for sem, val in fin.items():
            self.queue["sync"].append(([(sem, val)], None, None, 0))

        def run(E, items):
            for waits, fn, sem, inc in items:
                for s, v in waits:
                    E.wait_ge(sems[s], v)
                if fn is not None:
                    inst = fn(E)
                    if inc:
                        inst.then_inc(sems[sem], inc)

        for ename in _ENGINES:
            items = self.queue[ename]
            if not items:
                continue
            getattr(block, ename)(lambda E, it=items: run(E, it))


def _strips(w):
    O, I = w.shape
    t = np.asarray(w).T.reshape(I // 128, 128, O // 128, 128)
    return np.ascontiguousarray(t.transpose(2, 1, 0, 3).reshape(O // 128, 128, I))


def _pmaj(v):
    return np.ascontiguousarray(np.asarray(v).reshape(-1, 128).T)


def _rope_perm():
    p = np.arange(D).reshape(NH, DK)
    p = np.concatenate([p[:, 0::2], p[:, 1::2]], axis=1)
    return p.reshape(-1)


def _rope_tables():
    pos = np.arange(S, dtype=np.float64)
    k = np.arange(DK // 2, dtype=np.float64)
    ang = pos[None, :] / (10000.0 ** (2.0 * k[:, None] / DK))
    c32, s32 = np.cos(ang), np.sin(ang)
    C = np.tile(c32, (4, 1)).astype(np.float32).astype(BF16NP)
    Sg = np.concatenate([-s32, s32, -s32, s32], 0).astype(np.float32).astype(BF16NP)
    return C, Sg


def _pswap():
    P = np.zeros((128, 128), np.float32)
    for i in range(128):
        P[i, (i // 64) * 64 + (i % 64 + 32) % 64] = 1.0
    return P.astype(BF16NP)


def _l0_masks():
    p = np.arange(128)[:, None]
    j = np.arange(QR)[None, :]
    return [np.where(j < 128 * t + p, np.float32(-1e30), np.float32(0.0)).astype(np.float32)
            for t in range(4)]


def _l1_masks():
    sc = np.float32(1.0) / np.sqrt(np.float32(D * S))
    p = np.arange(128)[:, None]
    j = np.arange(QR)[None, :]
    return [np.where(j >= 128 * t + p, sc, np.float32(0.0)).astype(np.float32)
            for t in range(4)]


def _sel64():
    s = np.zeros((64, 128), np.float32)
    s[0, 0:64] = 1.0
    s[32, 64:128] = 1.0
    return s


def host_prepare(inp):
    d = {}
    perm = _rope_perm()
    d["wqT"] = _strips(np.asarray(inp["L0_attn_Wq"])[perm])
    d["wkT"] = _strips(np.asarray(inp["L0_attn_Wk"])[perm])
    d["wvT"] = np.ascontiguousarray(np.asarray(inp["L0_attn_Wv"]).T)
    d["woT"] = _strips(inp["L0_attn_Wo"])
    d["bq"] = np.asarray(inp["L0_attn_bq"])[perm].reshape(1, D).copy()
    d["bk"] = np.asarray(inp["L0_attn_bk"])[perm].reshape(1, D).copy()
    d["bv"] = np.asarray(inp["L0_attn_bv"]).reshape(1, D).copy()
    d["bo"] = np.asarray(inp["L0_attn_bo"]).reshape(1, D).copy()
    d["ropeC"], d["ropeS"] = _rope_tables()
    d["pswap"] = _pswap()
    p_ = np.arange(128)[:, None]
    j_ = np.arange(128)[None, :]
    d["tril128"] = np.where(j_ < p_, np.float32(-1e30), np.float32(0.0)).astype(np.float32)
    d["bvrow0"] = np.asarray(inp["L0_attn_bv"]).reshape(1, D)[:, 0:QR].copy()
    d["bvrow1"] = np.asarray(inp["L0_attn_bv"]).reshape(1, D)[:, QR:D].copy()
    d["sel64"] = _sel64()
    d["w1T"] = _strips(inp["L0_ffn_w1_w"])
    d["w2T"] = _strips(inp["L0_ffn_w2_w"])
    d["h1T"] = _strips(inp["L0_ffn_h1_w"])
    d["h2T"] = _strips(inp["L0_ffn_h2_w"])
    d["w3T"] = _strips(inp["L0_ffn_w3_w"])
    d["fs1"] = _pmaj(inp["L0_ffn_w1_s"])
    d["fs2"] = _pmaj(inp["L0_ffn_w2_s"])
    d["fsh1"] = _pmaj(inp["L0_ffn_h1_s"])
    d["fsh2"] = _pmaj(inp["L0_ffn_h2_s"])
    d["fs3"] = _pmaj(inp["L0_ffn_w3_s"])
    for nm in ("q", "k", "v"):
        d[f"w1{nm}T"] = _strips(np.asarray(inp[f"L1_attn_W1{nm}"]).T)
        d[f"w2{nm}T"] = _strips(np.asarray(inp[f"L1_attn_W2{nm}"]).T) if nm != "v" \
            else np.ascontiguousarray(np.asarray(inp["L1_attn_W2v"]))
        d[f"b1{nm}"] = np.asarray(inp[f"L1_attn_b1{nm}"]).reshape(1, D).copy()
        d[f"b2{nm}"] = np.asarray(inp[f"L1_attn_b2{nm}"]).reshape(1, D).copy()
    d["ln_w"] = _pmaj(inp["L1_attn_ln_w"])
    d["ln_b"] = _pmaj(inp["L1_attn_ln_b"])
    for t, m in enumerate(_l1_masks()):
        d[f"mask1_{t}"] = m
    d["qprojT"] = _strips(inp["L1_ffn_q_proj_w"])
    d["routerT"] = np.ascontiguousarray(np.asarray(inp["L1_ffn_router_w"]).T)
    d["akT"] = np.ascontiguousarray(np.asarray(inp["L1_ffn_aux_keys"]).T)
    d["auxv"] = np.ascontiguousarray(np.asarray(inp["L1_ffn_aux_values"]).reshape(32, 128, D))
    d["relm"] = np.asarray(inp["L1_ffn_reliability_mask"]).reshape(1, 4096).copy()
    d["uw1T"] = _strips(inp["L1_ffn_unc_w1"])
    d["uw2T"] = np.ascontiguousarray(np.asarray(inp["L1_ffn_unc_w2"]).reshape(256, 1))
    d["ub1"] = np.asarray(inp["L1_ffn_unc_b1"]).reshape(1, 256).copy()
    d["ub2"] = np.asarray(inp["L1_ffn_unc_b2"]).reshape(1, 1).copy()
    d["gw1"] = np.asarray(inp["L1_ffn_gate_w1"], np.float32).reshape(1, 1)
    d["gbias"] = np.asarray(inp["L1_ffn_gate_bias"], np.float32).reshape(1, 1)
    d["identf"] = np.eye(128, dtype=np.float32)
    return d


def _np_dt(a):
    return BF16 if a.dtype == BF16NP else F32


def declare_inputs(nc, host_arrays, names):
    aps = {}
    for n in names:
        a = host_arrays[n]
        aps[n] = nc.dram_tensor(n, list(a.shape), _np_dt(a), kind="ExternalInput").ap()
    return aps


class WS:
    pass


def make_consts(p, ctx):
    nc = p.nc
    c = WS()
    c.ones_row_bf = ctx.enter_context(nc.sbuf_tensor("c1r_bf", [1, QR], BF16))
    c.ones_col_bf = ctx.enter_context(nc.sbuf_tensor("c1c_bf", [128, 1], BF16))
    c.ones_row_f = ctx.enter_context(nc.sbuf_tensor("c1r_f", [1, 128], F32))
    for t in (c.ones_row_bf, c.ones_col_bf, c.ones_row_f):
        p.op("vector", lambda E, tt=t: E.memset(tt[:], 1.0), outs=[t])
    c._fcache = {}
    c._fctx = ctx
    c._prog = p

    def constf(val):
        val = float(val)
        if val not in c._fcache:
            t = c._fctx.enter_context(
                nc.sbuf_tensor(f"cf_{len(c._fcache)}", [128, 1], F32))
            c._prog.op("vector", lambda E, tt=t, v=val: E.memset(tt[:], v),
                       outs=[t])
            c._fcache[val] = t
        return c._fcache[val]

    c.constf = constf
    return c


def sbuf_const(p, ctx, name, dram_ap, dtype=None):
    shp = list(dram_ap.shape)
    assert len(shp) == 2
    dtype = dtype or dram_ap.dtype
    t = ctx.enter_context(p.nc.sbuf_tensor("k_" + name, shp, dtype))
    q = "gpsimd" if dtype != dram_ap.dtype else "scalar"
    p.dma(q, t[:], dram_ap, ins=[name + "@hbm"], outs=[t])
    return t


def emit_ln(p, ws, cst, x_dram, seq, xn_bf=None, xn32_dram=None, lv_acc=None,
            tag="ln", src_key="x@hbm", lv_slots=None, lv_dram=None,
            xnbf_dram=None):
    nc = p.nc
    rows = ws.rows
    ps_s, ps_q, ps_mu, ps_r = ws.ps[0], ws.ps[1], ws.ps[2], ws.ps[3]
    stats_only = xn_bf is None and xn32_dram is None and xnbf_dram is None
    for r in range(NQR):
        qs = seq * S + r * QR
        mu = rows[0:1, 0:QR]
        var = rows[0:1, QR:2 * QR]
        ex2 = var
        rstd = rows[0:1, 2 * QR:3 * QR]
        tmp = rstd
        onesf = cst.constf(1.0)
        for c in range(NCH):
            t32 = ws.x32[c % 2]
            sq = ws.sqf[c % 2]
            p.dma("sync", t32[:], x_dram[c * 128:(c + 1) * 128, qs:qs + QR],
                  ins=[src_key], outs=[t32])
            p.op("vector", lambda E, a=t32, o=sq: E.tensor_mul(o[:], a[:], a[:]),
                 ins=[t32], outs=[sq])
            p.op("tensor", lambda E, a=t32, st=(c == 0), sp=(c == NCH - 1):
                 E.matmul(ps_s[0:1, 0:QR], onesf[:], a[:], start=st, stop=sp),
                 ins=[t32, onesf], outs=[ps_s])
            p.op("tensor", lambda E, a=sq, st=(c == 0), sp=(c == NCH - 1):
                 E.matmul(ps_q[0:1, 0:QR], onesf[:], a[:], start=st, stop=sp),
                 ins=[sq, onesf], outs=[ps_q])
        RK = "rows"
        p.op("vector", lambda E: E.tensor_scalar_mul(mu, ps_s[0:1, 0:QR], 1.0 / D),
             ins=[ps_s], outs=[RK])
        p.op("vector", lambda E: E.tensor_scalar_mul(ex2, ps_q[0:1, 0:QR], 1.0 / D),
             ins=[ps_q], outs=[RK])
        p.op("vector", lambda E: E.tensor_mul(tmp, mu, mu), ins=[RK], outs=[RK])
        p.op("vector", lambda E: E.tensor_sub(var, ex2, tmp), ins=[RK], outs=[RK])
        eps_t = cst.constf(LN_EPS)
        mhalf = cst.constf(-0.5)
        p.op("scalar", lambda E: E.activation(tmp, var, AF.Ln, bias=eps_t[0:1, 0:1]),
             ins=[RK, eps_t], outs=[RK])
        p.op("scalar", lambda E: E.activation(rstd, tmp, AF.Exp, scale=mhalf[0:1, 0:1]),
             ins=[RK, mhalf], outs=[RK])
        if not stats_only:
            p.op("tensor", lambda E: E.matmul(ps_mu[:, 0:QR], cst.ones_row_f[0:1, :],
                                              mu, start=True, stop=True),
                 ins=[RK, cst.ones_row_f], outs=[ps_mu])
            p.op("tensor", lambda E: E.matmul(ps_r[:, 0:QR], cst.ones_row_f[0:1, :],
                                              rstd, start=True, stop=True),
                 ins=[RK, cst.ones_row_f], outs=[ps_r])
        if lv_acc is not None:
            p.op("vector", lambda E: E.tensor_scalar_add(tmp, var, LN_EPS),
                 ins=[RK, ps_r], outs=[RK])
            p.op("vector", lambda E: E.reciprocal(tmp, tmp), ins=[RK], outs=[RK])
            p.op("vector", lambda E: E.tensor_mul(tmp, var, tmp), ins=[RK], outs=[RK])
            if lv_slots is not None:
                onet = cst.constf(1.0)
                p.op("scalar", lambda E, b=onet: E.activation(tmp, tmp, AF.Ln, bias=b[0:1, 0:1]),
                     ins=[RK, onet], outs=[RK])
                p.op("vector", lambda E, sl=lv_slots[r]: E.reduce_sum(
                    sl, tmp, axis=AX.X), ins=[RK], outs=[lv_acc])
            else:
                onet = cst.constf(1.0)
                p.op("scalar", lambda E, b=onet:
                     E.activation(lv_acc[0:1, 0:QR], tmp, AF.Ln, bias=b[0:1, 0:1]),
                     ins=[RK, onet], outs=[lv_acc])
                tok = p.dma("sync", lv_dram[0:1, qs:qs + QR], lv_acc[0:1, 0:QR],
                            ins=[lv_acc], outs=[f"lv@hbm:{qs}"], sem="lvw")
                p.keep(tok)
        if stats_only:
            continue
        for c in range(NCH):
            t32 = ws.x32[2 + c % 2]
            p.dma("sync", t32[:], x_dram[c * 128:(c + 1) * 128, qs:qs + QR],
                  ins=[src_key], outs=[t32])
            p.op("vector", lambda E, a=t32: E.tensor_sub(a[:], a[:], ps_mu[:, 0:QR]),
                 ins=[t32, ps_mu], outs=[t32])
            if xn32_dram is not None:
                t2 = ws.xtmp
                p.op("vector", lambda E, a=t32, o=t2:
                     E.tensor_mul(o[:], a[:], ps_r[:, 0:QR]),
                     ins=[t32, ps_r], outs=[t2])
                if xn_bf is not None:
                    p.op("vector", lambda E, o=t2, dd=xn_bf[c], rr=r:
                         E.tensor_copy(dd[:, rr * QR:(rr + 1) * QR], o[:]),
                         ins=[t2], outs=[xn_bf[c]])
                tok = p.dma("sync", xn32_dram[c * 128:(c + 1) * 128, qs:qs + QR],
                            t2[:], ins=[t2], outs=[f"xn32@hbm:{qs}:{c}"], sem="xnw")
                p.keep(tok)
            elif xnbf_dram is not None:
                dd = ws.xnb[c]
                p.op("vector", lambda E, a=t32, o=dd:
                     E.tensor_mul(o, a[:], ps_r[:, 0:QR]),
                     ins=[t32, ps_r], outs=[ws.xnbt[c]])
                tok = p.dma("sync", xnbf_dram[c * 128:(c + 1) * 128, qs:qs + QR],
                            dd, ins=[ws.xnbt[c]],
                            outs=[f"xn0:{seq}:{r * QR}:{c}"], sem="xn0w")
                p.keep(tok)
            else:
                p.op("vector", lambda E, a=t32, dd=xn_bf[c], rr=r:
                     E.tensor_mul(dd[:, rr * QR:(rr + 1) * QR], a[:], ps_r[:, 0:QR]),
                     ins=[t32, ps_r], outs=[xn_bf[c]])


def emit_l0_qkv(p, ws, cst, seq, v_dram, q_dram):
    nc = p.nc
    p.dma("scalar", ws.ropeC, ws.d_ropeC, ins=["ropeC@hbm"], outs=[ws.habt[14]])
    p.dma("scalar", ws.ropeS, ws.d_ropeS, ins=["ropeS@hbm"], outs=[ws.habt[15]])
    for pi, (proj, wT_dram, b_dram, dst) in enumerate((
            ("q", ws.d_wqT, ws.d_bq, None), ("k", ws.d_wkT, ws.d_bk, ws.Kfm))):
        pre = wT_dram.tensor.dtype == BF16
        wkey = f"wT{proj}bf@hbm" if pre else f"wT{proj}@hbm"
        for m in range(NCH):
            wb = ws.wstrip[m % 2]
            p.dma("scalar" if pre else "gpsimd", wb[:, 0:D], wT_dram[m],
                  ins=[wkey], outs=[wb])
            bs = ws.bcur[m % 2]
            p.dma("gpsimd", bs[:], b_dram[0:1, m * 128:(m + 1) * 128],
                  ins=[f"b{proj}@hbm"], outs=[bs])
            for r in range(NQR):
                qs = r * QR
                ps = ws.ps[r % 2]
                for kc in range(NCH):
                    p.op("tensor", lambda E, w=wb, k2=kc, pp=ps, q=qs, st=(kc == 0):
                         E.matmul(pp[:, 0:QR], w[:, k2 * 128:(k2 + 1) * 128],
                                  ws.xn0[k2][:, q:q + QR], start=st, stop=False),
                         ins=[wb, ws.xn0[kc]], outs=[ps])
                p.op("tensor", lambda E, bb=bs, pp=ps:
                     E.matmul(pp[:, 0:QR], bb[:], cst.ones_row_bf[:],
                              start=False, stop=True),
                     ins=[bs, cst.ones_row_bf], outs=[ps])
                raw = ws.rraw[r % 2]
                uk = f"u{r % 2}"
                p.op("scalar", lambda E, pp=ps, o=raw: E.copy(out=o, in_=pp[:, 0:QR]),
                     ins=[ps], outs=[uk])
                psr = ws.ps[2 + r % 2]
                p.op("tensor", lambda E, o=psr, a=raw:
                     E.matmul(o[:, 0:QR], ws.pswap[:], a, start=True, stop=True),
                     ins=[uk, ws.pswap], outs=[psr])
                t1 = ws.rt1
                p.op("vector", lambda E, a=raw, t=qs, o=t1:
                     E.tensor_mul(o[:], ws.ropeC[:, t:t + QR], a),
                     ins=[uk, ws.habt[14]], outs=[t1])
                t2 = ws.rt2
                p.op("vector", lambda E, o=t2, pp=psr, t=qs:
                     E.tensor_mul(o[:], ws.ropeS[:, t:t + QR], pp[:, 0:QR]),
                     ins=[psr, ws.habt[15]], outs=[t2])
                if dst is None:
                    qstg = ws.u[2 + r % 2]
                    qk = f"u{2 + r % 2}"
                    p.op("vector", lambda E, o=qstg, a=t1, bb2=t2:
                         E.tensor_add(o, a[:], bb2[:]),
                         ins=[t1, t2], outs=[qk])
                    tok = p.dma("sync",
                                q_dram[m * 128:(m + 1) * 128, r * QR:(r + 1) * QR],
                                qstg, ins=[qk], outs=[f"q0:{m}:{r}"],
                                sem=f"qw{r % 2}")
                    p.keep(tok)
                else:
                    p.op("vector", lambda E, dd=dst[m], rr2=r, a=t1, bb2=t2:
                         E.tensor_add(dd[:, rr2 * QR:(rr2 + 1) * QR], a[:], bb2[:]),
                         ins=[t1, t2], outs=[dst[m]])
    for f in range(2):
        p.dma("gpsimd", ws.d_bvrow[f], ws.d_bvrow_dram[f],
              ins=[f"bvrow{f}@hbm"], outs=["rows"])
    vpre = ws.d_wvT.tensor.dtype == BF16
    for kc in range(NCH):
        wk = ws.wvk[kc]
        p.dma("scalar" if vpre else "gpsimd", ws.wvchunk[kc],
              ws.d_wvT[kc * 128:(kc + 1) * 128, :],
              ins=["wvTbf@hbm" if vpre else "wvT@hbm"],
              outs=wk if isinstance(wk, list) else [wk])
    for tt in range(S // 128):
        ts = tt * 128
        for fh in range(2):
            ps = ws.ps[4 + fh]
            vt = ws.ctx[2 * (tt % 2) + fh]
            for kc in range(NCH):
                wk = ws.wvk[kc]
                p.op("tensor", lambda E, k2=kc, t=ts, f=fh, pp=ps, st=(kc == 0):
                     E.matmul(pp[:, 0:QR], ws.xn0[k2][:, t:t + 128],
                              ws.wvchunk[k2][:, f * QR:(f + 1) * QR],
                              start=st, stop=False),
                     ins=[ws.xn0[kc]] + (wk if isinstance(wk, list) else [wk]),
                     outs=[ps])
            p.op("tensor", lambda E, f=fh, pp=ps, bb=ws.d_bvrow:
                 E.matmul(pp[:, 0:QR], cst.ones_row_bf[0:1, 0:128],
                          bb[f], start=False, stop=True),
                 ins=["rows", cst.ones_row_bf], outs=[ps])
            p.op("scalar", lambda E, pp=ps, dd=vt:
                 E.copy(out=dd[:], in_=pp[:, 0:QR]),
                 ins=[ps], outs=[vt])
            tok = p.dma("sync", v_dram[ts:ts + 128, fh * QR:(fh + 1) * QR], vt[:],
                        ins=[vt], outs=[f"v0:{tt}"], sem=f"vstg{2 * (tt % 2) + fh}")
            p.keep(tok)
    for i in range(4):
        p.barrier(f"v0all{i}", f"vstg{i}")


def gen_l0_attn(p, ws, cst, seq, s0_dram, v_dram, q_dram, dbg=None):
    nc = p.nc
    psC, psL = ws.ps[4], ws.ps[5]
    psF = psO = psL
    for r in range(NQR):
        qs = r * QR
        nkt = (r + 1) * 4
        p.dma("scalar", ws.qsb[0][:], q_dram[0:128, qs:qs + QR],
              ins=[f"q0:0:{r}"], outs=[ws.qsb[0]])
        for pr in range(NH // 2):
            qt = ws.qsb[pr]
            if pr + 1 < NH // 2:
                p.dma("scalar", ws.qsb[pr + 1][:],
                      q_dram[(pr + 1) * 128:(pr + 2) * 128, qs:qs + QR],
                      ins=[f"q0:{pr + 1}:{r}"], outs=[ws.qsb[pr + 1]])

            def emit_scores(kt):
                psA = ws.ps[kt % 2]
                psB = ws.ps[2 + kt % 2]
                uA = ws.u[kt % 2]
                uB = ws.u[2 + kt % 2]
                ks = kt * 128
                if pr % 2 == 0:
                    vbw = ws.vbuf[kt]
                    p.dma("scalar", vbw[:],
                          v_dram[ks:ks + 128, pr * 128:(pr + 2) * 128],
                          ins=["v0all0", "v0all1", "v0all2", "v0all3",
                               f"v0:{kt}"],
                          outs=[f"vb{kt}"])
                p.op("tensor", lambda E, pp=psA, c=pr, k=ks, q2=qt:
                     E.matmul(pp[:, 0:QR], ws.Kfm[c][0:64, k:k + 128],
                              q2[0:64, 0:QR], start=True, stop=True,
                              tile_position=(0, 0)),
                     ins=[ws.Kfm[pr], qt], outs=[psA])
                p.op("tensor", lambda E, pp=psB, c=pr, k=ks, q2=qt:
                     E.matmul(pp[:, 0:QR], ws.Kfm[c][64:128, k:k + 128],
                              q2[64:128, 0:QR], start=True, stop=True,
                              tile_position=(64, 0)),
                     ins=[ws.Kfm[pr], qt], outs=[psB])
                t = kt - 4 * r
                if t > 0:
                    for pp in (psA, psB):
                        p.op("vector", lambda E, p2=pp, t2=t:
                             E.tensor_scalar(p2[:, 0:128 * t2], p2[:, 0:128 * t2],
                                             -1e30, None, ALU.add),
                             ins=[pp], outs=[pp])
                if t >= 0:
                    for pp in (psA, psB):
                        p.op("vector", lambda E, p2=pp, t2=t:
                             E.tensor_add(p2[:, 128 * t2:128 * (t2 + 1)],
                                          p2[:, 128 * t2:128 * (t2 + 1)],
                                          ws.tril128[:]),
                             ins=[pp, ws.tril128], outs=[pp])
                qsc = cst.constf(0.125)
                p.op("scalar", lambda E, pp=psA, o=uA, sc=qsc:
                     E.activation(out=o[:], in_=pp[:, 0:QR], func=AF.Exp, scale=sc[:, 0:1]),
                     ins=[psA, qsc], outs=[f"u{kt % 2}"])
                p.op("scalar", lambda E, pp=psB, o=uB, sc=qsc:
                     E.activation(out=o[:], in_=pp[:, 0:QR], func=AF.Exp, scale=sc[:, 0:1]),
                     ins=[psB, qsc], outs=[f"u{2 + kt % 2}"])

            def emit_av(kt):
                uA = ws.u[kt % 2]
                uB = ws.u[2 + kt % 2]
                ukA, ukB = f"u{kt % 2}", f"u{2 + kt % 2}"
                vco = (pr % 2) * 128
                vbA = ws.vbuf[kt][:, vco:vco + 64]
                vbB = ws.vbuf[kt][:, vco + 64:vco + 128]
                st = kt == 0
                sp = kt == nkt - 1
                p.op("tensor", lambda E, v=vbA, a=uA, st2=st, sp2=sp:
                     E.matmul(psC[0:64, 0:QR], v, a[:],
                              start=st2, stop=sp2, tile_position=(0, 0)),
                     ins=[f"vb{kt}", ukA], outs=[psC])
                p.op("tensor", lambda E, v=vbB, a=uB, st2=st, sp2=sp:
                     E.matmul(psC[64:128, 0:QR], v, a[:],
                              start=st2, stop=sp2, tile_position=(0, 64)),
                     ins=[f"vb{kt}", ukB], outs=[psC])
                p.op("tensor", lambda E, a=uA, st2=st, sp2=sp:
                     E.matmul(psL[0:1, 0:QR], cst.ones_col_bf[:], a[:],
                              start=st2, stop=sp2, tile_position=(0, 0)),
                     ins=[ukA, cst.ones_col_bf], outs=[psL])
                p.op("tensor", lambda E, a=uB, st2=st, sp2=sp:
                     E.matmul(psL[32:33, 0:QR], cst.ones_col_bf[:], a[:],
                              start=st2, stop=sp2, tile_position=(0, 32)),
                     ins=[ukB, cst.ones_col_bf], outs=[psL])

            for kt in range(nkt):
                emit_scores(kt)
                if kt > 0:
                    yield
                    emit_av(kt - 1)
            yield
            emit_av(nkt - 1)
            yield
            ft = ws.ftile64
            p.op("vector", lambda E: E.tensor_scalar(ft, psL[0:64, 0:QR],
                                                     ATT_EPS, None, ALU.add),
                 ins=[psL], outs=[ws.xf4])
            p.op("vector", lambda E: E.reciprocal(ft, ft),
                 ins=[ws.xf4], outs=[ws.xf4])
            p.op("tensor", lambda E: E.matmul(psF[:, 0:QR], ws.sel64[:], ft,
                                              start=True, stop=True),
                 ins=[ws.xf4, ws.sel64], outs=[psF])
            fsb = ws.fsb
            p.op("vector", lambda E: E.tensor_copy(fsb, psF[:, 0:QR]),
                 ins=[psF], outs=[ws.xf5])
            p.op("vector", lambda E, c=pr: E.tensor_mul(ws.ctx[c][:], psC[:, 0:QR], fsb),
                 ins=[psC, ws.xf5], outs=[ws.ctx[pr]])
            yield
        opre = ws.d_woT.tensor.dtype == BF16
        for m in range(NCH):
            wo = ws.wobuf[m % 2]
            p.dma("scalar" if opre else "gpsimd", wo[:], ws.d_woT[m],
                  ins=["woTbf@hbm" if opre else "woT@hbm"], outs=[wo])
            bs = ws.bcur[m % 2]
            p.dma("gpsimd", bs[:], ws.d_bo[0:1, m * 128:(m + 1) * 128],
                  ins=["bo@hbm"], outs=[bs])
            for kc in range(NCH):
                p.op("tensor", lambda E, w=wo, k2=kc, st=(kc == 0):
                     E.matmul(psO[:, 0:QR], w[:, k2 * 128:(k2 + 1) * 128],
                              ws.ctx[k2][:], start=st, stop=False),
                     ins=[wo, ws.ctx[kc]], outs=[psO])
            p.op("tensor", lambda E, bb=bs:
                 E.matmul(psO[:, 0:QR], bb[:], cst.ones_row_bf[:],
                          start=False, stop=True),
                 ins=[bs, cst.ones_row_bf], outs=[psO])
            so = ws.s0t[m % 2]
            p.op("scalar", lambda E, o=so: E.copy(out=o[:], in_=psO[:, 0:QR]),
                 ins=[psO], outs=[so])
            tok = p.dma("scalar", s0_dram[m * 128:(m + 1) * 128, seq * S + qs:seq * S + qs + QR],
                        so[:], ins=[so], outs=[f"s0@hbm:{qs}:{m}"],
                        sem=f"s0w{qs}p{m % 2}")
            p.keep(tok)
            if m == NCH - 1:
                p.barrier(f"s0all:{qs}:0", f"s0w{qs}p0")
                p.barrier(f"s0all:{qs}:1", f"s0w{qs}p1")
            yield


def gen_l0_ffn(p, ws, cst, seq, x_dram, s0_dram, x1_dram, dbg=None):
    nc = p.nc
    psS, psQ = ws.ps[4], ws.ps[5]
    NT = S // T_FFN
    HH = DH // 2

    def dump(name, tile, key):
        tk = p.dma("sync", dbg[name], tile, ins=[key], outs=["dbg_" + name],
                   sem="dbgw")
        p.keep(tk)
    for tt in range(NT):
        ts = tt * T_FFN
        for m in range(32):
            wa = ws.wsmall[m % 2]
            wb2 = ws.wsmall[2 + m % 2]
            p.dma("sync", wa[:], ws.d_w1T[m], ins=["w1Tbf@hbm"], outs=[wa])
            p.dma("sync", wb2[:], ws.d_w2T[m], ins=["w2Tbf@hbm"], outs=[wb2])
            ps1, ps2 = ws.ps[6], ws.ps[7]
            for kc in range(NCH):
                p.op("tensor", lambda E, w=wa, k2=kc, pp=ps1, t=ts, st=(kc == 0):
                     E.matmul(pp[:, 0:QR], w[:, k2 * 128:(k2 + 1) * 128],
                              ws.xn0[k2][:, t:t + QR], start=st, stop=(k2 == NCH - 1)),
                     ins=[wa, ws.xn0[kc]], outs=[ps1])
            for kc in range(NCH):
                p.op("tensor", lambda E, w=wb2, k2=kc, pp=ps2, t=ts, st=(kc == 0):
                     E.matmul(pp[:, 0:QR], w[:, k2 * 128:(k2 + 1) * 128],
                              ws.xn0[k2][:, t:t + QR], start=st, stop=(k2 == NCH - 1)),
                     ins=[wb2, ws.xn0[kc]], outs=[ps2])
            tsil, tlin = ws.tsil, ws.tlin
            p.op("scalar", lambda E, pp=ps1, mm=m, o=tsil:
                 E.activation(out=o, in_=pp[:, 0:QR], func=AF.Silu,
                              scale=ws.fs1[:, mm:mm + 1]),
                 ins=[ps1, ws.fs1], outs=[ws.x32[2]])
            p.op("scalar", lambda E, pp=ps2, mm=m, o=tlin:
                 E.activation(out=o, in_=pp[:, 0:QR], func=AF.Copy,
                              scale=ws.fs2[:, mm:mm + 1]),
                 ins=[ps2, ws.fs2], outs=[ws.x32[2]])
            p.op("vector", lambda E, mm=m: E.tensor_mul(ws.ha[mm], tsil, tlin),
                 ins=[ws.x32[2]], outs=[ws.habt[m // 4]])
            if dbg is not None and tt == 0 and m == 31:
                for i in range(8):
                    dump(f"hab{i}", ws.habt[i][:], ws.habt[i])
            yield
        for stage, dW, sc_t, src_of, dst_of in (
                (2, ws.d_h1T, ws.fsh1, 0, 8), (3, ws.d_h2T, ws.fsh2, 8, 0)):
            for m in range(32):
                ps1 = ws.ps[6 + m % 2]
                wl = ws.wlarge[m % 2]
                p.dma("sync", wl[:], dW[m], ins=[f"h{stage}Tbf@hbm"], outs=[wl])
                for kc in range(32):
                    p.op("tensor", lambda E, w=wl, k2=kc, pp=ps1, st=(kc == 0),
                         so=src_of:
                         E.matmul(pp[:, 0:QR], w[:, k2 * 128:(k2 + 1) * 128],
                                  ws.hsrc[so + k2 // 4][:, (k2 % 4) * QR:(k2 % 4 + 1) * QR],
                                  start=st, stop=(k2 == 31)),
                         ins=[wl, ws.habt[src_of + kc // 4]], outs=[ps1])
                if dbg is not None and tt == 0 and m == 0 and stage == 2:
                    stg2 = ws.s0t[1]
                    p.op("vector", lambda E, o=stg2, pp=ps1:
                         E.tensor_copy(o[:], pp[:, 0:QR]), ins=[ps1], outs=[stg2])
                    dump("ps2m0", stg2[:], stg2)
                p.op("scalar", lambda E, pp=ps1, mm=m, d2=dst_of, sct=sc_t:
                     E.activation(out=ws.hsrc[d2 + mm // 4][:, (mm % 4) * QR:(mm % 4 + 1) * QR],
                                  in_=pp[:, 0:QR], func=AF.Gelu,
                                  scale=sct[:, mm:mm + 1]),
                     ins=[ps1, sc_t], outs=[ws.habt[dst_of + m // 4]])
                if dbg is not None and tt == 0 and m == 0:
                    dump(f"hs{stage}", ws.hsrc[dst_of][:, 0:QR],
                         ws.habt[dst_of])
                yield
        for m in range(NCH):
            ps1 = ws.ps[6 + m % 2]
            wl = ws.wlarge[m % 2]
            p.dma("sync", wl[:], ws.d_w3T[m], ins=["w3Tbf@hbm"], outs=[wl])
            for kc in range(32):
                p.op("tensor", lambda E, w=wl, k2=kc, pp=ps1, st=(kc == 0):
                     E.matmul(pp[:, 0:QR], w[:, k2 * 128:(k2 + 1) * 128],
                              ws.hsrc[k2 // 4][:, (k2 % 4) * QR:(k2 % 4 + 1) * QR],
                              start=st, stop=(k2 == 31)),
                     ins=[wl, ws.habt[kc // 4]], outs=[ps1])
            g = ws.gtile
            p.op("scalar", lambda E, pp=ps1, mm=m, o=g:
                 E.activation(out=o[:], in_=pp[:, 0:QR], func=AF.Silu,
                              scale=ws.fs3[:, mm:mm + 1]),
                 ins=[ps1, ws.fs3], outs=[g])
            if dbg is not None and tt == 0 and m == 0:
                dump("g0", g[:], g)
            s0i = ws.s0t[m % 2]
            xi = ws.x32[m % 2]
            off = seq * S + ts
            assert f"s0all:{ts}:0" in p.writer, \
                f"stage4 tile {ts} emitted before its s0 barrier (tune ratio)"
            p.dma("sync", s0i[:], s0_dram[m * 128:(m + 1) * 128, off:off + QR],
                  ins=[f"s0all:{ts}:0", f"s0all:{ts}:1", f"s0@hbm:{ts}:{m}"],
                  outs=[s0i])
            p.dma("sync", xi[:], x_dram[m * 128:(m + 1) * 128, off:off + QR],
                  ins=["x@hbm"], outs=[xi])
            p.op("vector", lambda E, a=g, b2=s0i: E.tensor_mul(a[:], a[:], b2[:]),
                 ins=[g, s0i], outs=[g])
            p.op("vector", lambda E, a=g, b2=xi: E.tensor_add(a[:], a[:], b2[:]),
                 ins=[xi, g], outs=[g])
            tok = p.dma("sync", x1_dram[m * 128:(m + 1) * 128, off:off + QR],
                        g[:], ins=[g], outs=[f"x1@hbm:{off}:{m}"], sem="x1w")
            p.keep(tok)
            onesf = cst.constf(1.0)
            sqx = ws.x32[2 + m % 2]
            p.op("vector", lambda E, a=g, o=sqx: E.tensor_mul(o[:], a[:], a[:]),
                 ins=[g], outs=[sqx])
            p.op("tensor", lambda E, a=g, st=(m == 0), sp=(m == NCH - 1):
                 E.matmul(psS[0:1, 0:QR], onesf[:], a[:], start=st, stop=sp),
                 ins=[g, onesf], outs=[psS])
            p.op("tensor", lambda E, a=sqx, st=(m == 0), sp=(m == NCH - 1):
                 E.matmul(psQ[0:1, 0:QR], onesf[:], a[:], start=st, stop=sp),
                 ins=[sqx, onesf], outs=[psQ])
            yield
        RK = "rows"
        mu = ws.rows[0:1, 0:QR]
        var = ws.rows[0:1, QR:2 * QR]
        tmp = ws.rows[0:1, 2 * QR:3 * QR]
        p.op("vector", lambda E: E.tensor_scalar_mul(mu, psS[0:1, 0:QR], 1.0 / D),
             ins=[psS], outs=[RK])
        p.op("vector", lambda E: E.tensor_scalar_mul(var, psQ[0:1, 0:QR], 1.0 / D),
             ins=[psQ], outs=[RK])
        p.op("vector", lambda E: E.tensor_mul(tmp, mu, mu), ins=[RK], outs=[RK])
        p.op("vector", lambda E: E.tensor_sub(var, var, tmp), ins=[RK], outs=[RK])
        p.op("vector", lambda E: E.tensor_scalar_add(tmp, var, LN_EPS),
             ins=[RK], outs=[RK])
        p.op("vector", lambda E: E.reciprocal(tmp, tmp), ins=[RK], outs=[RK])
        p.op("vector", lambda E: E.tensor_mul(tmp, var, tmp), ins=[RK], outs=[RK])
        onet = cst.constf(1.0)
        p.op("scalar", lambda E, b=onet: E.activation(tmp, tmp, AF.Ln, bias=b[0:1, 0:1]),
             ins=[RK, onet], outs=[RK])
        p.op("vector", lambda E, sl=ws.lvsum[0:1, seq * 4 + tt:seq * 4 + tt + 1]:
             E.reduce_sum(sl, tmp, axis=AX.X), ins=[RK], outs=["lvacc"])
        yield


A_INPUT_NAMES = ["wqT", "wkT", "wvT", "woT", "bq", "bk", "bv", "bo", "ropeC",
                 "ropeS", "pswap", "tril128", "sel64", "w1T", "w2T", "h1T",
                 "h2T", "w3T", "fs1", "fs2", "fsh1", "fsh2", "fs3", "bvrow0",
                 "bvrow1"]


def build_ws_A(p, stack, dr):
    nc = p.nc
    e = stack.enter_context
    cst = make_consts(p, stack)
    ws = WS()
    ws.ps = [e(nc.psum_tensor(f"pp{i}", [128, QR], F32)) for i in range(8)]
    p.exclusive.update(t.name for t in ws.ps)
    ws.x32 = [e(nc.sbuf_tensor(f"x32_{i}", [128, QR], F32)) for i in range(5)]
    ws.rows = e(nc.sbuf_tensor("rows", [1, QR * 3], F32))
    ws.ftile64 = ws.x32[4][0:64, 0:QR]
    ws.xf4 = ws.x32[4]
    ws.gtile = ws.x32[4]
    ws.s0t = [e(nc.sbuf_tensor(f"s0t{i}", [128, QR], F32)) for i in range(2)]
    ws.fsb = ws.s0t[1][:]
    ws.xf5 = ws.s0t[1]
    ws.sqf = [ws.s0t[1], ws.s0t[0]]
    ws.rt1 = ws.s0t[0][:]
    ws.rt2 = ws.s0t[1][:]
    ws.tsil = ws.x32[2][:].bitcast(BF16)[:, 0:QR]
    ws.tlin = ws.x32[2][:].bitcast(BF16)[:, QR:2 * QR]
    ws.xn0 = [e(nc.sbuf_tensor(f"xn0_{i}", [128, S], BF16)) for i in range(NCH)]
    ws.scr = [e(nc.sbuf_tensor(f"scr{i}", [128, S], BF16)) for i in range(NCH)]
    ws.Kfm = ws.scr
    ws.qsb = [e(nc.sbuf_tensor(f"qsb{i}", [128, QR], BF16)) for i in range(NCH)]
    import os as _os
    _hw = 4 if _os.environ.get("A_TEST_SMALL_HABT") else 4 * QR
    ws.habt = [e(nc.sbuf_tensor(f"hab{i}", [128, _hw if i < 14 else 4 * QR], BF16))
               for i in range(16)]
    ws.hsrc = ws.habt
    if _hw == 4 * QR:
        ws.ha = [ws.habt[m // 4][:, (m % 4) * QR:(m % 4 + 1) * QR] for m in range(32)]
        ws.hb = [ws.habt[8 + m // 4][:, (m % 4) * QR:(m % 4 + 1) * QR] for m in range(32)]
    ws.ropeC = ws.habt[14][:]
    ws.ropeS = ws.habt[15][:]
    ws.ut = e(nc.sbuf_tensor("ut", [128, 4 * QR], BF16))
    ws.u = [ws.ut[:, i * QR:(i + 1) * QR] for i in range(4)]
    ws.rraw = [ws.u[0], ws.u[1]]
    ws.wvchunk = [ws.x32[i][:].bitcast(BF16) for i in range(5)] +                  [None, ws.ut[:, 0:D], ws.ut[:, D:2 * D]]
    ws.wvk = [ws.x32[i] for i in range(5)] + [None, ["u0", "u1"], ["u2", "u3"]]
    ws.vbuf = [e(nc.sbuf_tensor(f"vbuf{i}", [128, 256], BF16)) for i in range(16)]
    ws.ctx = [e(nc.sbuf_tensor(f"ctx{i}", [128, QR], BF16)) for i in range(NCH)]
    ws.wsmall = [e(nc.sbuf_tensor(f"wsm{i}", [128, D], BF16)) for i in range(4)]
    ws.wstrip = ws.wsmall[:2]
    ws.wobuf = [e(nc.sbuf_tensor(f"wob{i}", [128, D], BF16)) for i in range(2)]
    ws.wvchunk[5] = ws.wobuf[0][:]
    ws.wvk[5] = ws.wobuf[0]
    ws.wlarge = [e(nc.sbuf_tensor(f"wlg{i}", [128, DH], BF16)) for i in range(2)]
    ws.lvsum = e(nc.sbuf_tensor("lvsum", [1, 8], F32))
    _bc = e(nc.sbuf_tensor("bc0", [1, 128], BF16))
    ws.bcur = [_bc, _bc]
    ws.d_bq, ws.d_bk = dr["bq"], dr["bk"]
    ws.d_bv, ws.d_bo = dr["bv"], dr["bo"]
    ws.d_ropeC, ws.d_ropeS = dr["ropeC"], dr["ropeS"]
    ws.d_bvrow_dram = [dr[f"bvrow{f}"] for f in range(2)]
    ws.d_bvrow = [ws.rows[:].bitcast(BF16)[0:1, f * QR:(f + 1) * QR]
                  for f in range(2)]
    ws.pswap = sbuf_const(p, stack, "pswap", dr["pswap"])
    ws.tril128 = sbuf_const(p, stack, "tril128", dr["tril128"], BF16)
    ws.sel64 = sbuf_const(p, stack, "sel64", dr["sel64"])
    ws.fs1 = sbuf_const(p, stack, "fs1", dr["fs1"])
    ws.fs2 = sbuf_const(p, stack, "fs2", dr["fs2"])
    ws.fsh1 = sbuf_const(p, stack, "fsh1", dr["fsh1"])
    ws.fsh2 = sbuf_const(p, stack, "fsh2", dr["fsh2"])
    ws.fs3 = sbuf_const(p, stack, "fs3", dr["fs3"])
    ws.d_wqT, ws.d_wkT, ws.d_wvT = dr["wqT"], dr["wkT"], dr["wvT"]
    ws.d_woT = dr["woT"]
    ws.d_w1T, ws.d_w2T = dr["w1T"], dr["w2T"]
    ws.d_h1T, ws.d_h2T, ws.d_w3T = dr["h1T"], dr["h2T"], dr["w3T"]
    return cst, ws


def build_launch_A(host_meta, ffn_per_attn=1):
    import contextlib
    nc = bass.Bass("TRN2", target_bir_lowering=False)
    p = Prog(nc)
    dr = declare_inputs(nc, host_meta, A_INPUT_NAMES)
    x_dram = nc.dram_tensor("xfm", [D, TOK], F32, kind="ExternalInput").ap()
    x1_dram = nc.dram_tensor("x1fm", [D, TOK], F32, kind="ExternalOutput").ap()
    lv_dram = nc.dram_tensor("lvpart", [1, 1], F32, kind="ExternalOutput").ap()
    s0_dram = nc.dram_tensor("s0fm", [D, TOK], F32).ap()
    v_dram = nc.dram_tensor("v0tm", [S, D], BF16).ap()
    q_dram = nc.dram_tensor("q0fm", [D, S], BF16).ap()
    wbf_shapes = (("w1T", 32, D), ("w2T", 32, D), ("h1T", 32, DH),
                  ("h2T", 32, DH), ("w3T", 8, DH))
    wbf = {}
    for nm, n, inner in wbf_shapes:
        wbf[nm] = nc.dram_tensor(nm + "bf", [n, 128, inner], BF16).ap()

    stack = contextlib.ExitStack()
    with stack:
        cst, ws = build_ws_A(p, stack, dr)
        ws.d_w1T, ws.d_w2T = wbf["w1T"], wbf["w2T"]
        ws.d_h1T, ws.d_h2T, ws.d_w3T = wbf["h1T"], wbf["h2T"], wbf["w3T"]
        att_bf = {}
        for nm in ("wqT", "wkT", "woT"):
            t = nc.dram_tensor(nm + "bf", [NCH, 128, D], BF16).ap()
            for m in range(NCH):
                p.dma("gpsimd", t[m], dr[nm][m], ins=[nm + "@hbm"],
                      outs=[f"{nm}bf:{m}"], sem="wcast0")
            att_bf[nm] = t
        wvbf = nc.dram_tensor("wvTbf", [D, D], BF16).ap()
        p.dma("gpsimd", wvbf, dr["wvT"], ins=["wvT@hbm"], outs=["wvTbf:0"],
              sem="wcast0")
        for key in ("wTqbf@hbm", "wTkbf@hbm", "woTbf@hbm", "wvTbf@hbm"):
            p.writer[key] = ("wcast0", p.cnt["wcast0"], True, "gpsimd")
        ws.d_wqT, ws.d_wkT = att_bf["wqT"], att_bf["wkT"]
        ws.d_woT, ws.d_wvT = att_bf["woT"], wvbf
        for seq in range(SEQ_PER_CORE):
            emit_ln(p, ws, cst, x_dram, seq, xn_bf=ws.xn0, tag=f"ln0s{seq}",
                    src_key="x@hbm")
            emit_l0_qkv(p, ws, cst, seq, v_dram, q_dram)
            if seq == 0:
                for nm, n, inner in wbf_shapes:
                    for m in range(n):
                        p.dma("gpsimd", wbf[nm][m], dr[nm][m], ins=[nm + "@hbm"],
                              outs=[f"{nm}bf:{m}"], sem="wcast")
                for nm, _n, _i in wbf_shapes:
                    p.writer[nm + "bf@hbm"] = ("wcast", p.cnt["wcast"], True,
                                               "gpsimd")
            ag = gen_l0_attn(p, ws, cst, seq, s0_dram, v_dram, q_dram)
            fg = gen_l0_ffn(p, ws, cst, seq, x_dram, s0_dram, x1_dram)
            a_done = f_done = False
            for _ in range(48):
                if not a_done:
                    a_done = next(ag, "END") == "END"
            while not (a_done and f_done):
                if not a_done:
                    a_done = next(ag, "END") == "END"
                n_f = ffn_per_attn if not a_done else 10 ** 9
                for _ in range(n_f):
                    if f_done:
                        break
                    f_done = next(fg, "END") == "END"
        lvtot = ws.rows[0:1, 0:1]
        p.op("vector", lambda E: E.reduce_sum(lvtot, ws.lvsum[0:1, :], axis=AX.X),
             ins=["lvacc", ws.lvsum], outs=["rows"])
        tok = p.dma("sync", lv_dram, lvtot, ins=["rows"], outs=["lv@hbm"])
        p.keep(tok)
        p.materialize(stack)
    return nc


B_INPUT_NAMES = ["w1qT", "w2qT", "w1kT", "w2kT", "w1vT", "w2vT",
                 "b1q", "b2q", "b1k", "b2k", "b1v", "b2v", "ln_w", "ln_b",
                 "mask1_0", "mask1_1", "mask1_2", "mask1_3", "qprojT",
                 "routerT", "akT", "auxv", "relm", "uw1T", "uw2T", "ub1",
                 "ub2", "gw1", "gbias", "identf"]
L1_SCALE = float(np.float32(1.0) / np.sqrt(np.float32(D * S)))
GMEAN_N = float(B * S)


def build_ws_B(p, stack, dr):
    nc = p.nc
    e = stack.enter_context
    cst = make_consts(p, stack)
    ws = WS()
    ws.ps = [e(nc.psum_tensor(f"pp{i}", [128, QR], F32)) for i in range(8)]
    p.exclusive.update(t.name for t in ws.ps)
    ws.x32 = [e(nc.sbuf_tensor(f"x32_{i}", [128, QR], F32)) for i in range(NCH)]
    ws.xtmp = e(nc.sbuf_tensor("xtmp", [128, QR], F32))
    ws.rows = e(nc.sbuf_tensor("rows", [1, QR * 4], F32))
    ws.lvB = e(nc.sbuf_tensor("lvB", [1, QR], F32))
    ws.T1 = [e(nc.sbuf_tensor(f"T1_{i}", [128, S], BF16)) for i in range(NCH)]
    ws.QB = [e(nc.sbuf_tensor(f"QB_{i}", [128, S], BF16)) for i in range(NCH)]
    ws.KB = [e(nc.sbuf_tensor(f"KB_{i}", [128, S], BF16)) for i in range(NCH)]
    ws.xbf = [ws.T1[0][:, j * QR:(j + 1) * QR] for j in range(2)]
    ws.sq = [ws.T1[1][:, j * QR:(j + 1) * QR] for j in range(2)]
    ws.varB = [e(nc.sbuf_tensor(f"varB{i}", [128, 2 * DH], BF16)) for i in range(2)]
    ws.Vtm = [ws.varB[i // 8][:, (i % 8) * D:(i % 8 + 1) * D] for i in range(S // 128)]
    ws.upA = e(nc.sbuf_tensor("upA", [128, 2 * DH], BF16))
    ws.uprime = [ws.upA[:, i * QR:(i + 1) * QR] for i in range(16)]
    ws.auxv_tiles = [(ws.T1 + ws.QB)[n // 2][:, (n % 2) * D:(n % 2 + 1) * D]
                     for n in range(32)]
    ws.e_nm = [ws.KB[i // 4][:, (i % 4) * QR:(i % 4 + 1) * QR] for i in range(32)]
    ws.scores_tm = ws.varB[0][:].bitcast(F32)[:, 0:4096]
    ws.rel_bcast = ws.upA[:].bitcast(F32)[:, 0:4096]
    ws.ctx32 = ws.x32
    ws.rkbf = e(nc.sbuf_tensor("rkbf", [48, 4096], BF16))
    ws.qf = ws.x32[5][:].bitcast(BF16)[:, 0:QR]
    ws.rq = e(nc.sbuf_tensor("rq", [48, QR], BF16))
    ws.h1bf = [ws.x32[4][:].bitcast(BF16)[:, j * QR:(j + 1) * QR] for j in range(2)]
    ws.tp8 = e(nc.sbuf_tensor("tp8", [128, 128], F32))
    ws.m1row = ws.rows[0:1, QR:2 * QR]
    ws.t8row = ws.rows[0:1, 2 * QR:3 * QR]
    ws.zrow = ws.rows[0:1, 3 * QR:4 * QR]
    ws.grow = ws.x32[7][0:1, 0:QR]
    ws.m1b = e(nc.sbuf_tensor("m1b", [128, QR], F32))
    ws.d8b = e(nc.sbuf_tensor("d8b", [128, QR], F32))
    ws.geffb = e(nc.sbuf_tensor("geffb", [128, QR], F32))
    ws.sqf = [ws.xtmp, ws.geffb]
    ws.enm32 = ws.xtmp
    ws.enme = ws.x32[0][:].bitcast(BF16)[:, 0:QR]
    ws.enmm = ws.x32[0][:].bitcast(BF16)[:, QR:2 * QR]
    ws.gsc = e(nc.sbuf_tensor("gsc", [1, 8], F32))
    ws.wsm = [e(nc.sbuf_tensor(f"wsmB{i}", [128, D], BF16)) for i in range(4)]
    ws.w2vch = [ws.upA[:, (i * D):(i + 1) * D] for i in range(8)]
    ws.relrow = ws.rows[0:1, QR:2 * QR]
    ws.b1cur = e(nc.sbuf_tensor("b1cur", [1, D], BF16))
    ws.b2cur = e(nc.sbuf_tensor("b2cur", [1, D], BF16))
    ws.d_b1 = {n: dr[f"b1{n}"] for n in "qkv"}
    ws.d_b2 = {n: dr[f"b2{n}"] for n in "qkv"}
    ws.lnw = sbuf_const(p, stack, "ln_w", dr["ln_w"])
    ws.lnb = sbuf_const(p, stack, "ln_b", dr["ln_b"])
    ws.mask1 = [sbuf_const(p, stack, f"mask1_{t}", dr[f"mask1_{t}"], BF16) for t in range(4)]
    ws.routerT = sbuf_const(p, stack, "routerT", dr["routerT"], BF16)
    ws.uw2sb = e(nc.sbuf_tensor("uw2sb", [128, 2], BF16))
    for j in range(2):
        p.dma("gpsimd", ws.uw2sb[:, j:j + 1], dr["uw2T"][j * 128:(j + 1) * 128, :],
              ins=["uw2T@hbm"], outs=[ws.uw2sb])
    ws.ub1 = sbuf_const(p, stack, "ub1", dr["ub1"], BF16)
    ws.ub2 = sbuf_const(p, stack, "ub2", dr["ub2"])
    ws.gw1 = sbuf_const(p, stack, "gw1", dr["gw1"])
    ws.gbias = sbuf_const(p, stack, "gbias", dr["gbias"])
    ws.identf = sbuf_const(p, stack, "identf", dr["identf"])
    ws.d_w1T = {n: dr[f"w1{n}T"] for n in "qkv"}
    ws.d_w2T = {n: dr[f"w2{n}T"] for n in "qkv"}
    ws.d_qprojT = dr["qprojT"]
    ws.d_akT = dr["akT"]
    ws.d_auxv = dr["auxv"]
    ws.d_relm = dr["relm"]
    ws.d_uw1T = dr["uw1T"]
    return cst, ws


def emit_l1_mlp(p, ws, cst, seq, xn1_dram):
    nc = p.nc
    for proj in "qkv":
        b1, b2 = ws.b1cur, ws.b2cur
        p.dma("gpsimd", b1[:], ws.d_b1[proj], ins=[f"b1{proj}@hbm"], outs=[b1])
        p.dma("gpsimd", b2[:], ws.d_b2[proj], ins=[f"b2{proj}@hbm"], outs=[b2])
        for r in range(NQR):
            ts = seq * S + r * QR
            xnb = [ws.x32[kc // 2][:].bitcast(BF16)[:, (kc % 2) * QR:(kc % 2 + 1) * QR]
                   for kc in range(NCH)]
            for kc in range(NCH):
                p.dma("gpsimd", xnb[kc],
                      xn1_dram[kc * 128:(kc + 1) * 128, ts:ts + QR],
                      ins=["xn1@hbm"], outs=[ws.x32[kc // 2]])
            for m in range(NCH):
                wb = ws.wsm[m % 2]
                p.dma("gpsimd", wb[:], ws.d_w1T[proj][m],
                      ins=[f"w1{proj}@hbm"], outs=[wb])
                ps = ws.ps[m % 2]
                for kc in range(NCH):
                    p.op("tensor", lambda E, w=wb, k2=kc, x=xnb[kc], pp=ps, st=(kc == 0):
                         E.matmul(pp[:, 0:QR], w[:, k2 * 128:(k2 + 1) * 128],
                                  x, start=st, stop=False),
                         ins=[wb, ws.x32[kc // 2]], outs=[ps])
                p.op("tensor", lambda E, bb=b1, mm=m, pp=ps:
                     E.matmul(pp[:, 0:QR], bb[0:1, mm * 128:(mm + 1) * 128],
                              cst.ones_row_bf[:], start=False, stop=True),
                     ins=[b1, cst.ones_row_bf], outs=[ps])
                p.op("scalar", lambda E, pp=ps, mm=m, rr=r:
                     E.activation(out=ws.T1[mm][:, rr * QR:(rr + 1) * QR],
                                  in_=pp[:, 0:QR], func=AF.Gelu),
                     ins=[ps], outs=[ws.T1[m]])
        if proj in "qk":
            dst = ws.QB if proj == "q" else ws.KB
            for r in range(NQR):
                for m in range(NCH):
                    wb = ws.wsm[2 + m % 2]
                    p.dma("gpsimd", wb[:], ws.d_w2T[proj][m],
                          ins=[f"w2{proj}@hbm"], outs=[wb])
                    ps = ws.ps[2 + m % 2]
                    for kc in range(NCH):
                        p.op("tensor", lambda E, w=wb, k2=kc, rr=r, pp=ps, st=(kc == 0):
                             E.matmul(pp[:, 0:QR], w[:, k2 * 128:(k2 + 1) * 128],
                                      ws.T1[k2][:, rr * QR:(rr + 1) * QR],
                                      start=st, stop=False),
                             ins=[wb, ws.T1[kc]], outs=[ps])
                    p.op("tensor", lambda E, bb=b2, mm=m, pp=ps:
                         E.matmul(pp[:, 0:QR], bb[0:1, mm * 128:(mm + 1) * 128],
                                  cst.ones_row_bf[:], start=False, stop=True),
                         ins=[b2, cst.ones_row_bf], outs=[ps])
                    p.op("scalar", lambda E, pp=ps, mm=m, rr=r, dd=dst[m]:
                         E.activation(out=dd[:, rr * QR:(rr + 1) * QR],
                                      in_=pp[:, 0:QR], func=AF.Copy),
                         ins=[ps], outs=[dst[m]])
        else:
            for kc in range(NCH):
                p.dma("gpsimd", ws.w2vch[kc], ws.d_w2T["v"][kc * 128:(kc + 1) * 128, :],
                      ins=["w2v@hbm"], outs=[ws.upA])
            for tt in range(S // 128):
                ts = tt * 128
                for fh in range(2):
                    ps = ws.ps[4 + fh]
                    for kc in range(NCH):
                        p.op("tensor", lambda E, k2=kc, t=ts, f=fh, pp=ps, st=(kc == 0):
                             E.matmul(pp[:, 0:QR], ws.T1[k2][:, t:t + 128],
                                      ws.w2vch[k2][:, f * QR:(f + 1) * QR],
                                      start=st, stop=False),
                             ins=[ws.T1[kc], ws.upA], outs=[ps])
                    p.op("tensor", lambda E, f=fh, pp=ps:
                         E.matmul(pp[:, 0:QR], cst.ones_row_bf[0:1, 0:128],
                                  b2[0:1, f * QR:(f + 1) * QR], start=False, stop=True),
                         ins=[b2, cst.ones_row_bf], outs=[ps])
                    p.op("scalar", lambda E, pp=ps, dd=ws.Vtm[tt], f=fh:
                         E.copy(out=dd[:, f * QR:(f + 1) * QR], in_=pp[:, 0:QR]),
                         ins=[ps], outs=[ws.varB[tt // 8]])


def emit_l1_attn(p, ws, cst, seq, s1_dram):
    nc = p.nc
    sc_t = cst.constf(L1_SCALE)
    for r in range(NQR):
        qs = r * QR
        nkt = (r + 1) * 4
        for kt in range(nkt):
            ps = ws.ps[kt % 2]
            ks = kt * 128
            for kc in range(NCH):
                p.op("tensor", lambda E, k2=kc, k=ks, q=qs, pp=ps, st=(kc == 0):
                     E.matmul(pp[:, 0:QR], ws.KB[k2][:, k:k + 128],
                              ws.QB[k2][:, q:q + QR], start=st, stop=(k2 == NCH - 1)),
                     ins=[ws.KB[kc], ws.QB[kc]], outs=[ps])
            t = kt - 4 * r
            if t >= 0:
                p.op("vector", lambda E, pp=ps, m=ws.mask1[t], u=ws.uprime[kt]:
                     E.tensor_mul(u, pp[:, 0:QR], m[:]),
                     ins=[ps, ws.mask1[t]], outs=[ws.upA])
            else:
                p.op("scalar", lambda E, pp=ps, u=ws.uprime[kt], sc=sc_t:
                     E.activation(out=u, in_=pp[:, 0:QR], func=AF.Copy,
                                  scale=sc[:, 0:1]),
                     ins=[ps, sc_t], outs=[ws.upA])
        for hc in range(NCH):
            ps = ws.ps[4 + hc % 2]
            for kt in range(nkt):
                p.op("tensor", lambda E, k2=kt, h=hc, pp=ps, st=(kt == 0):
                     E.matmul(pp[:, 0:QR], ws.Vtm[k2][:, h * 128:(h + 1) * 128],
                              ws.uprime[k2], start=st, stop=(k2 == nkt - 1)),
                     ins=[ws.varB[kt // 8], ws.upA], outs=[ps])
            p.op("vector", lambda E, pp=ps, o=ws.ctx32[hc]:
                 E.tensor_copy(o[:], pp[:, 0:QR]), ins=[ps], outs=[ws.ctx32[hc]])
        RK = "rows"
        mu = ws.rows[0:1, 0:QR]
        var = ws.rows[0:1, QR:2 * QR]
        rstd = ws.rows[0:1, 2 * QR:3 * QR]
        tmp = ws.rows[0:1, 3 * QR:4 * QR]
        ps_s, ps_q = ws.ps[6], ws.ps[7]
        onesf = cst.constf(1.0)
        for hc in range(NCH):
            sqt = ws.xtmp
            p.op("vector", lambda E, a=ws.ctx32[hc], o=sqt: E.tensor_mul(o[:], a[:], a[:]),
                 ins=[ws.ctx32[hc]], outs=[sqt])
            p.op("tensor", lambda E, a=ws.ctx32[hc], st=(hc == 0), sp=(hc == NCH - 1):
                 E.matmul(ps_s[0:1, 0:QR], onesf[:], a[:], start=st, stop=sp),
                 ins=[ws.ctx32[hc], onesf], outs=[ps_s])
            p.op("tensor", lambda E, a=sqt, st=(hc == 0), sp=(hc == NCH - 1):
                 E.matmul(ps_q[0:1, 0:QR], onesf[:], a[:], start=st, stop=sp),
                 ins=[sqt, onesf], outs=[ps_q])
        p.op("vector", lambda E: E.tensor_scalar_mul(mu, ps_s[0:1, 0:QR], 1.0 / D),
             ins=[ps_s], outs=[RK])
        p.op("vector", lambda E: E.tensor_scalar_mul(var, ps_q[0:1, 0:QR], 1.0 / D),
             ins=[ps_q], outs=[RK])
        p.op("vector", lambda E: E.tensor_mul(tmp, mu, mu), ins=[RK], outs=[RK])
        p.op("vector", lambda E: E.tensor_sub(var, var, tmp), ins=[RK], outs=[RK])
        eps_t = cst.constf(LN_EPS)
        mhalf = cst.constf(-0.5)
        p.op("scalar", lambda E, b=eps_t: E.activation(tmp, var, AF.Ln, bias=b[0:1, 0:1]),
             ins=[RK, eps_t], outs=[RK])
        p.op("scalar", lambda E, s2=mhalf: E.activation(rstd, tmp, AF.Exp, scale=s2[0:1, 0:1]),
             ins=[RK, mhalf], outs=[RK])
        ps_mu, ps_r = ws.ps[0], ws.ps[1]
        p.op("tensor", lambda E: E.matmul(ps_mu[:, 0:QR], cst.ones_row_f[0:1, :],
                                          mu, start=True, stop=True),
             ins=[RK, cst.ones_row_f], outs=[ps_mu])
        p.op("tensor", lambda E: E.matmul(ps_r[:, 0:QR], cst.ones_row_f[0:1, :],
                                          rstd, start=True, stop=True),
             ins=[RK, cst.ones_row_f], outs=[ps_r])
        for hc in range(NCH):
            a = ws.ctx32[hc]
            p.op("vector", lambda E, t2=a: E.tensor_sub(t2[:], t2[:], ps_mu[:, 0:QR]),
                 ins=[a, ps_mu], outs=[a])
            p.op("vector", lambda E, t2=a: E.tensor_mul(t2[:], t2[:], ps_r[:, 0:QR]),
                 ins=[a, ps_r], outs=[a])
            p.op("vector", lambda E, t2=a, h=hc: E.tensor_scalar(
                t2[:], t2[:], ws.lnw[:, h:h + 1], ws.lnb[:, h:h + 1],
                ALU.mult, ALU.add),
                 ins=[a, ws.lnw, ws.lnb], outs=[a])
            tok = p.dma("sync", s1_dram[hc * 128:(hc + 1) * 128,
                                        seq * S + qs:seq * S + qs + QR],
                        a[:], ins=[a], outs=[f"s1@hbm:{seq}:{qs}:{hc}"], sem="s1w")
            p.keep(tok)


def emit_l1_ffn_prep(p, ws, cst):
    nc = p.nc
    for ns in range(8):
        ak = ws.wsm[ns % 2]
        p.dma("gpsimd", ak[:, 0:QR], ws.d_akT[:, ns * QR:(ns + 1) * QR],
              ins=["akT@hbm"], outs=[ak])
        ps = ws.ps[ns % 2]
        p.op("tensor", lambda E, a=ak, pp=ps:
             E.matmul(pp[0:48, 0:QR], ws.routerT[:], a[:, 0:QR],
                      start=True, stop=True),
             ins=[ak, ws.routerT], outs=[ps])
        p.op("scalar", lambda E, pp=ps, n=ns:
             E.copy(out=ws.rkbf[:, n * QR:(n + 1) * QR], in_=pp[0:48, 0:QR]),
             ins=[ps], outs=[ws.rkbf])
    for ns in range(8):
        p.dma("sync", ws.relrow, ws.d_relm[0:1, ns * QR:(ns + 1) * QR],
              ins=["relm@hbm"], outs=["rows"])
        ps = ws.ps[2 + ns % 2]
        p.op("tensor", lambda E, pp=ps:
             E.matmul(pp[:, 0:QR], cst.ones_row_f[0:1, :], ws.relrow,
                      start=True, stop=True),
             ins=["rows", cst.ones_row_f], outs=[ps])
        p.op("vector", lambda E, pp=ps, n=ns:
             E.tensor_copy(ws.rel_bcast[:, n * QR:(n + 1) * QR], pp[:, 0:QR]),
             ins=[ps], outs=[ws.upA])
    g = ws.gsc
    p.op("vector", lambda E: E.reduce_sum(g[0:1, 0:1], ws.gparts_sb[0:1, :], axis=AX.X),
         ins=[ws.gparts_sb], outs=[g])
    p.op("vector", lambda E: E.tensor_scalar_mul(g[0:1, 1:2], g[0:1, 0:1], 1.0 / GMEAN_N),
         ins=[g], outs=[g])
    p.op("vector", lambda E: E.tensor_scalar_add(g[0:1, 1:2], g[0:1, 1:2], 1e-6),
         ins=[g], outs=[g])
    p.op("vector", lambda E: E.reciprocal(g[0:1, 2:3], g[0:1, 1:2]),
         ins=[g], outs=[g])
    p.op("vector", lambda E: E.tensor_scalar_mul(g[0:1, 3:4], g[0:1, 2:3], 0.5),
         ins=[g], outs=[g])
    for n in range(32):
        p.dma("gpsimd", ws.auxv_tiles[n], ws.d_auxv[n],
              ins=["auxv@hbm"], outs=[(ws.T1 + ws.QB)[n // 2]])


def emit_l1_ffn(p, ws, cst, seq, xn1_dram, x1_dram, s1_dram, lv_dram, x2_dram):
    nc = p.nc
    for r in range(NQR):
        ts = seq * S + r * QR
        xnb = [ws.x32[kc // 2][:].bitcast(BF16)[:, (kc % 2) * QR:(kc % 2 + 1) * QR]
               for kc in range(NCH)]
        for kc in range(NCH):
            p.dma("gpsimd", xnb[kc], xn1_dram[kc * 128:(kc + 1) * 128, ts:ts + QR],
                  ins=["xn1@hbm"], outs=[ws.x32[kc // 2]])
        ps = ws.ps[0]
        wqp = ws.wsm[0]
        p.dma("gpsimd", wqp[:], ws.d_qprojT[0], ins=["qproj@hbm"], outs=[wqp])
        for kc in range(NCH):
            p.op("tensor", lambda E, k2=kc, x=xnb[kc], pp=ps, w=wqp, st=(kc == 0):
                 E.matmul(pp[:, 0:QR], w[:, k2 * 128:(k2 + 1) * 128], x,
                          start=st, stop=(k2 == NCH - 1)),
                 ins=[wqp, ws.x32[kc // 2]], outs=[ps])
        p.op("scalar", lambda E, pp=ps: E.copy(out=ws.qf, in_=pp[:, 0:QR]),
             ins=[ps], outs=[ws.x32[5]])
        ps = ws.ps[1]
        p.op("tensor", lambda E, pp=ps:
             E.matmul(pp[0:48, 0:QR], ws.routerT[:], ws.qf, start=True, stop=True),
             ins=[ws.routerT, ws.x32[5]], outs=[ps])
        rdsc = cst.constf(RD_SCALE)
        p.op("scalar", lambda E, pp=ps, sc=rdsc:
             E.activation(out=ws.rq[:], in_=pp[0:48, 0:QR], func=AF.Copy,
                          scale=sc[0:48, 0:1]),
             ins=[ps, rdsc], outs=[ws.rq])
        for m in range(2):
            ps = ws.ps[2 + m]
            wb = ws.wsm[2]
            p.dma("gpsimd", wb[:], ws.d_uw1T[m], ins=["uw1@hbm"], outs=[wb])
            for kc in range(NCH):
                p.op("tensor", lambda E, k2=kc, x=xnb[kc], pp=ps, w=wb, st=(kc == 0):
                     E.matmul(pp[:, 0:QR], w[:, k2 * 128:(k2 + 1) * 128], x,
                              start=st, stop=False),
                     ins=[wb, ws.x32[kc // 2]], outs=[ps])
            p.op("tensor", lambda E, pp=ps, mm=m:
                 E.matmul(pp[:, 0:QR], ws.ub1[0:1, mm * 128:(mm + 1) * 128],
                          cst.ones_row_bf[:], start=False, stop=True),
                 ins=[ws.ub1, cst.ones_row_bf], outs=[ps])
            p.op("scalar", lambda E, pp=ps, o=ws.h1bf[m]:
                 E.activation(out=o, in_=pp[:, 0:QR], func=AF.Gelu),
                 ins=[ps], outs=[ws.x32[4]])
        ps = ws.ps[4]
        for m in range(2):
            p.op("tensor", lambda E, mm=m, pp=ps, st=(m == 0):
                 E.matmul(pp[0:1, 0:QR], ws.uw2sb[:, mm:mm + 1], ws.h1bf[mm],
                          start=st, stop=(mm == 1)),
                 ins=[ws.uw2sb, ws.x32[4]], outs=[ps])
        sig = ws.grow
        p.op("scalar", lambda E, pp=ps:
             E.activation(out=sig, in_=pp[0:1, 0:QR], func=AF.Sigmoid,
                          bias=ws.ub2[0:1, 0:1]),
             ins=[ps, ws.ub2], outs=[ws.x32[7]])
        for tt in range(4):
            tq = tt * 128
            for ns in range(8):
                ps = ws.ps[5 + ns % 2]
                p.op("tensor", lambda E, n=ns, t2=tq, pp=ps:
                     E.matmul(pp[:, 0:QR], ws.rq[:, t2:t2 + 128],
                              ws.rkbf[:, n * QR:(n + 1) * QR], start=True, stop=True),
                     ins=[ws.rq, ws.rkbf], outs=[ps])
                p.op("vector", lambda E, n=ns, pp=ps:
                     E.tensor_add(ws.scores_tm[:, n * QR:(n + 1) * QR], pp[:, 0:QR],
                                  ws.rel_bcast[:, n * QR:(n + 1) * QR]),
                     ins=[ps, ws.upA], outs=[ws.varB[0]])
            p.op("vector", lambda E: E.max(out=ws.tp8[:, 0:8], in_=ws.scores_tm[:]),
                 ins=[ws.varB[0]], outs=[ws.tp8])
            p.op("vector", lambda E: E.tensor_scalar_mul(ws.tp8[:, 8:9], ws.tp8[:, 0:1], -1.0),
                 ins=[ws.tp8], outs=[ws.tp8])
            p.op("scalar", lambda E: E.activation(out=ws.tp8[:, 16:24], in_=ws.tp8[:, 0:8],
                                                  func=AF.Exp, bias=ws.tp8[:, 8:9]),
                 ins=[ws.tp8], outs=[ws.tp8])
            p.op("vector", lambda E: E.reduce_sum(ws.tp8[:, 24:25], ws.tp8[:, 16:24], axis=AX.X),
                 ins=[ws.tp8], outs=[ws.tp8])
            p.op("vector", lambda E: E.tensor_copy(ws.tp8[:, 32:33], ws.tp8[:, 7:8]),
                 ins=[ws.tp8], outs=[ws.tp8])
            p.op("vector", lambda E: E.reciprocal(ws.tp8[:, 64:65], ws.tp8[:, 24:25]),
                 ins=[ws.tp8], outs=[ws.tp8])
            psT = ws.ps[7]
            p.op("tensor", lambda E, pp=psT:
                 E.transpose(pp[:, 0:128], ws.tp8[:], ws.identf[:]),
                 ins=[ws.tp8, ws.identf], outs=[psT])
            for col, row in ((0, ws.m1row), (32, ws.t8row), (64, ws.zrow)):
                p.op("vector", lambda E, c2=col, pp=psT, rw=row, t2=tq:
                     E.tensor_copy(rw[0:1, t2:t2 + 128], pp[c2:c2 + 1, 0:128]),
                     ins=[psT], outs=["rows"])
        p.op("vector", lambda E: E.tensor_sub(ws.t8row, ws.t8row, ws.m1row),
             ins=["rows"], outs=["rows"])
        ps = ws.ps[6]
        p.op("tensor", lambda E, pp=ps:
             E.matmul(pp[:, 0:QR], cst.ones_row_f[0:1, :], ws.m1row,
                      start=True, stop=True),
             ins=["rows", cst.ones_row_f], outs=[ps])
        p.op("vector", lambda E, pp=ps: E.tensor_copy(ws.m1b[:], pp[:, 0:QR]),
             ins=[ps], outs=[ws.m1b])
        p.op("tensor", lambda E, pp=ps:
             E.matmul(pp[:, 0:QR], cst.ones_row_f[0:1, :], ws.t8row,
                      start=True, stop=True),
             ins=["rows", cst.ones_row_f], outs=[ps])
        p.op("vector", lambda E, pp=ps: E.tensor_copy(ws.d8b[:], pp[:, 0:QR]),
             ins=[ps], outs=[ws.d8b])
        for n in range(32):
            ps = ws.ps[n % 2]
            p.op("tensor", lambda E, n2=n, pp=ps:
                 E.matmul(pp[:, 0:QR], ws.rkbf[:, n2 * 128:(n2 + 1) * 128],
                          ws.rq[:], start=True, stop=True),
                 ins=[ws.rkbf, ws.rq], outs=[ps])
            p.op("vector", lambda E, n2=n, pp=ps:
                 E.tensor_scalar(ws.enm32[:], pp[:, 0:QR],
                                 ws.relnm[:, n2:n2 + 1], None, ALU.add),
                 ins=[ps, ws.relnm], outs=[ws.xtmp])
            p.op("vector", lambda E: E.tensor_sub(ws.enm32[:], ws.enm32[:], ws.m1b[:]),
                 ins=[ws.xtmp, ws.m1b], outs=[ws.xtmp])
            p.op("scalar", lambda E: E.activation(out=ws.enme, in_=ws.enm32[:],
                                                  func=AF.Exp),
                 ins=[ws.xtmp], outs=[ws.x32[0]])
            p.op("vector", lambda E: E.tensor_tensor(ws.enmm, ws.enm32[:],
                                                     ws.d8b[:], ALU.is_ge),
                 ins=[ws.xtmp, ws.d8b], outs=[ws.x32[0]])
            p.op("gpsimd", lambda E, n2=n: E.tensor_mul(
                ws.e_nm[n2], ws.enme, ws.enmm),
                 ins=[ws.x32[0]], outs=[ws.KB[n // 4]])
        lvr = ws.rows[0:1, 0:QR]
        p.dma("sync", lvr, lv_dram[0:1, ts:ts + QR], ins=["lv@hbm"], outs=["rows"])
        p.op("vector", lambda E: E.tensor_scalar(lvr, lvr, ws.gsc[0:1, 3:4], None,
                                                 ALU.mult),
             ins=["rows", ws.gsc], outs=["rows"])
        p.op("vector", lambda E: E.tensor_scalar_mul(ws.grow, ws.grow, 2.5),
             ins=[ws.x32[7]], outs=[ws.x32[7]])
        p.op("vector", lambda E: E.tensor_add(lvr, lvr, ws.grow),
             ins=["rows", ws.x32[7]], outs=["rows"])
        p.op("vector", lambda E: E.tensor_scalar_min(lvr, lvr, 5.0),
             ins=["rows"], outs=["rows"])
        p.op("vector", lambda E: E.tensor_scalar_max(lvr, lvr, 0.0),
             ins=["rows"], outs=["rows"])
        p.op("scalar", lambda E: E.activation(out=lvr, in_=lvr, func=AF.Sigmoid,
                                              scale=ws.gw1[0:1, 0:1],
                                              bias=ws.gbias[0:1, 0:1]),
             ins=["rows", ws.gw1, ws.gbias], outs=["rows"])
        p.op("vector", lambda E: E.tensor_mul(lvr, lvr, ws.zrow),
             ins=["rows"], outs=["rows"])
        ps = ws.ps[6]
        p.op("tensor", lambda E, pp=ps:
             E.matmul(pp[:, 0:QR], cst.ones_row_f[0:1, :], lvr,
                      start=True, stop=True),
             ins=["rows", cst.ones_row_f], outs=[ps])
        p.op("vector", lambda E, pp=ps: E.tensor_copy(ws.geffb[:], pp[:, 0:QR]),
             ins=[ps], outs=[ws.geffb])
        for hc in range(NCH):
            ps = ws.ps[2 + hc % 2]
            for n in range(32):
                p.op("tensor", lambda E, n2=n, h=hc, pp=ps, st=(n == 0):
                     E.matmul(pp[:, 0:QR], ws.auxv_tiles[n2][:, h * 128:(h + 1) * 128],
                              ws.e_nm[n2], start=st, stop=(n2 == 31)),
                     ins=[(ws.T1 + ws.QB)[n // 2], ws.KB[n // 4]], outs=[ps])
            a = ws.x32[4 + hc % 2]
            b2t = ws.x32[6 + hc % 2]
            p.op("vector", lambda E, pp=ps, o=a: E.tensor_mul(o[:], pp[:, 0:QR], ws.geffb[:]),
                 ins=[ps, ws.geffb], outs=[a])
            p.dma("sync", b2t[:], xn1_dram[hc * 128:(hc + 1) * 128, ts:ts + QR],
                  ins=["xn1@hbm"], outs=[b2t])
            p.op("vector", lambda E, o=a, x=b2t: E.tensor_add(o[:], o[:], x[:]),
                 ins=[a, b2t], outs=[a])
            p.op("scalar", lambda E, o=a: E.activation(out=o[:], in_=o[:], func=AF.Silu),
                 ins=[a], outs=[a])
            p.dma("sync", b2t[:], s1_dram[hc * 128:(hc + 1) * 128, ts:ts + QR],
                  ins=["s1@hbm"], outs=[b2t])
            p.op("vector", lambda E, o=a, x=b2t: E.tensor_mul(o[:], o[:], x[:]),
                 ins=[a, b2t], outs=[a])
            p.dma("sync", b2t[:], x1_dram[hc * 128:(hc + 1) * 128, ts:ts + QR],
                  ins=["x1@hbm"], outs=[b2t])
            p.op("vector", lambda E, o=a, x=b2t: E.tensor_add(o[:], o[:], x[:]),
                 ins=[a, b2t], outs=[a])
            tok = p.dma("sync", x2_dram[hc * 128:(hc + 1) * 128, ts:ts + QR],
                        a[:], ins=[a], outs=[f"x2@hbm:{ts}:{hc}"], sem="x2w")
            p.keep(tok)


def build_launch_B(host_meta, phases=("ln", "mlp", "attn", "prep", "ffn"),
                   debug=False):
    import contextlib
    nc = bass.Bass("TRN2", target_bir_lowering=False)
    p = Prog(nc)
    dr = declare_inputs(nc, host_meta, B_INPUT_NAMES)
    kind = "ExternalOutput" if debug else "Internal"
    x1_dram = nc.dram_tensor("x1fm", [D, TOK], F32, kind="ExternalInput").ap()
    gparts = nc.dram_tensor("gparts", [1, NCORES], F32, kind="ExternalInput").ap()
    x2_dram = nc.dram_tensor("x2fm", [D, TOK], F32, kind="ExternalOutput").ap()
    xn1_dram = nc.dram_tensor("xn1fm", [D, TOK], F32, kind=kind).ap()
    s1_dram = nc.dram_tensor("s1fm", [D, TOK], F32, kind=kind).ap()
    lv_dram = nc.dram_tensor("lvrows", [1, TOK], F32, kind=kind).ap()

    stack = contextlib.ExitStack()
    with stack:
        e = stack.enter_context
        cst, ws = build_ws_B(p, stack, dr)
        ws.gparts_sb = e(nc.sbuf_tensor("gparts_sb", [1, NCORES], F32))
        p.dma("sync", ws.gparts_sb[:], gparts, ins=["gparts@hbm"],
              outs=[ws.gparts_sb])
        ws.relnm = e(nc.sbuf_tensor("relnm", [128, 32], F32))
        p.dma("sync", ws.relnm[:],
              dr["relm"].rearrange("o (n p) -> p (o n)", p=128),
              ins=["relm@hbm"], outs=[ws.relnm],
              allow_slow_non_contiguous=True)
        if "ln" in phases:
            for seq in range(SEQ_PER_CORE):
                emit_ln(p, ws, cst, x1_dram, seq, xn_bf=None, xn32_dram=xn1_dram,
                        lv_acc=ws.lvB, tag=f"ln1s{seq}", src_key="x1@hbm",
                        lv_dram=lv_dram)
            p.barrier("xn1@hbm", "xnw")
            p.barrier("lv@hbm", "lvw")
        for seq in range(SEQ_PER_CORE):
            if "mlp" in phases:
                emit_l1_mlp(p, ws, cst, seq, xn1_dram)
            if "attn" in phases:
                emit_l1_attn(p, ws, cst, seq, s1_dram)
        if "attn" in phases:
            p.barrier("s1@hbm", "s1w")
        if "prep" in phases:
            emit_l1_ffn_prep(p, ws, cst)
        if "ffn" in phases:
            for seq in range(SEQ_PER_CORE):
                emit_l1_ffn(p, ws, cst, seq, xn1_dram, x1_dram, s1_dram, lv_dram,
                            x2_dram)
        p.materialize(stack)
    return nc


_host_cache = {}


def kernel(**inputs):
    meta = host_prepare(inputs)
    x = np.asarray(inputs["L0_attn_x"], np.float32)

    in_maps = []
    for c in range(NCORES):
        m = {k: v for k, v in meta.items()}
        xc = np.ascontiguousarray(
            x[c * SEQ_PER_CORE:(c + 1) * SEQ_PER_CORE].reshape(TOK, D).T)
        m["xfm"] = xc
        in_maps.append(m)

    ncA = build_launch_A(meta)
    resA = run_bass_kernel_spmd(ncA, in_maps, core_ids=list(range(NCORES)))
    parts = np.stack([np.asarray(resA.results[c]["lvpart"]).reshape(1)
                      for c in range(NCORES)], axis=1)

    in_mapsB = []
    for c in range(NCORES):
        m = {k: v for k, v in meta.items()}
        m["x1fm"] = np.asarray(resA.results[c]["x1fm"])
        m["gparts"] = np.ascontiguousarray(parts)
        in_mapsB.append(m)

    ncB = build_launch_B(meta)
    resB = run_bass_kernel_spmd(ncB, in_mapsB, core_ids=list(range(NCORES)))

    out = np.empty((B, S, D), np.float32)
    for c in range(NCORES):
        x2 = np.asarray(resB.results[c]["x2fm"])
        out[c * SEQ_PER_CORE:(c + 1) * SEQ_PER_CORE] = \
            x2.T.reshape(SEQ_PER_CORE, S, D)
    return out
